# version 09
# baseline (speedup 1.0000x reference)
import os
from collections import defaultdict
from contextlib import ExitStack

import numpy as np
import ml_dtypes

import concourse.bass as bass
import concourse.mybir as mybir
from concourse.bass_utils import run_bass_kernel_spmd

F32 = mybir.dt.float32
BF16 = mybir.dt.bfloat16
AF = mybir.ActivationFunctionType
ALU = mybir.AluOpType
AX = mybir.AxisListType

NCORE = 8
B, S, D = 2, 8192, 1024
H, DH = 16, 64
TCORE = S * B // NCORE
NT = TCORE // 128
PRE = 3 * TCORE
NPT = PRE // 128
E, INTER = 16, 512
EPS = 1e-5
ROPE_BASE = 10000.0

STAGE = os.environ.get("KSTAGE", "full")
KRET = os.environ.get("KRET", "d")
KOWNT = int(os.environ.get("KOWNT", str(NT)))

SBUF_BASE = 16512
SBUF_END = 229376


class Sched:
    ENG = ("pe", "dve", "act", "pool", "sp")
    NLANES = 8

    def __init__(self):
        self.ops = []
        self.last_writer = {}
        self.readers = defaultdict(list)
        self.lane_use = defaultdict(int)

    def add(self, engine, fn, reads=(), writes=(), dma=False, tag="", extra_deps=(),
            slow=False):
        idx = len(self.ops)
        deps = set(extra_deps)
        for r in reads:
            w = self.last_writer.get(r)
            if w is not None:
                deps.add(w)
        for r in writes:
            w = self.last_writer.get(r)
            if w is not None:
                deps.add(w)
            for rd in self.readers.get(r, ()):
                deps.add(rd)
        op = dict(idx=idx, eng=engine, fn=fn, deps=deps, dma=dma,
                  tag=tag, need_inc=False, inc_after=None, sem=engine,
                  lane_wait=None, slow=slow, vc={}, extra_vc=None)
        if dma:
            nq = self.lane_use[engine]
            lane = nq % self.NLANES
            use = nq // self.NLANES + 1
            self.lane_use[engine] = nq + 1
            op["sem"] = f"{engine}.q{lane}"
            op["need_inc"] = True
            op["inc_after"] = 16 * use
            if use > 1:
                op["lane_wait"] = (op["sem"], 16 * (use - 1))
        self.ops.append(op)
        for r in reads:
            self.readers[r].append(idx)
        for r in writes:
            self.last_writer[r] = idx
            self.readers[r] = [idx]
        return idx

    def barrier(self):
        last = {}
        for op in self.ops:
            last[op["eng"]] = op["idx"]
        lane_vc = {}
        for qe, nq in self.lane_use.items():
            for q in range(self.NLANES):
                uses = nq // self.NLANES + (1 if q < nq % self.NLANES else 0)
                if uses:
                    lane_vc[f"{qe}.q{q}"] = 16 * uses
        for e in self.ENG:
            deps = tuple(v for k, v in last.items() if k != e)
            i = self.add(e, lambda eng: None, extra_deps=deps, tag=f"bar_{e}")
            self.ops[i]["extra_vc"] = dict(lane_vc)

    def finalize(self):
        for op in self.ops:
            for d in op["deps"]:
                if d != op["idx"]:
                    self.ops[d]["need_inc"] = True
        cnt = defaultdict(int)
        for op in self.ops:
            if op["dma"]:
                continue
            if op["need_inc"]:
                cnt[op["eng"]] += 1
                op["inc_after"] = cnt[op["eng"]]
        self.final_counts = dict(cnt)
        for op in self.ops:
            vc = {}
            for d in sorted(op["deps"]):
                if d == op["idx"]:
                    continue
                dop = self.ops[d]
                for sname, v in dop["vc"].items():
                    if vc.get(sname, 0) < v:
                        vc[sname] = v
                if dop["inc_after"] is not None:
                    if vc.get(dop["sem"], 0) < dop["inc_after"]:
                        vc[dop["sem"]] = dop["inc_after"]
            if op["extra_vc"]:
                for sname, v in op["extra_vc"].items():
                    if vc.get(sname, 0) < v:
                        vc[sname] = v
            op["vc"] = vc

    def run_engine(self, engine_name, eng, sems):
        seen = defaultdict(int)
        for op in self.ops:
            if op["eng"] != engine_name:
                continue
            for sname, v in sorted(op["vc"].items()):
                if sname == engine_name:
                    continue
                if v > seen[sname]:
                    eng.wait_ge(sems[sname], v)
                    seen[sname] = v
            if engine_name == "dve":
                v = op["vc"].get("dve", 0)
                if v > seen["dve"]:
                    eng.wait_ge(sems["dve"], v)
                    seen["dve"] = v
            if op["lane_wait"] is not None:
                sname, v = op["lane_wait"]
                if v > seen[sname]:
                    eng.wait_ge(sems[sname], v)
                    seen[sname] = v
            inst = op["fn"](eng)
            if op["need_inc"]:
                assert inst is not None, op["tag"]
                inst.then_inc(sems[op["sem"]], 16 if op["dma"] else 1)


class Arena:
    def __init__(self, nc, base=SBUF_BASE):
        self.nc = nc
        self.off = base
        self.hi = base

    def alloc(self, name, shape, dtype, align=64):
        per_part = int(np.prod(shape[1:])) * mybir.dt.size(dtype)
        self.off = (self.off + align - 1) // align * align
        t = self.nc.alloc_sbuf_tensor_at(name, list(shape), dtype, offset=self.off)
        self.off += per_part
        self.hi = max(self.hi, self.off)
        assert self.off <= SBUF_END, f"SBUF overflow at {name}: {self.off}"
        return t


def build_nc():
    nc = bass.Bass("TRN2", target_bir_lowering=False)

    def din(name, shape, dt=F32):
        return nc.dram_tensor(name, list(shape), dt, kind="ExternalInput").ap()

    x_own = din("x_own", [TCORE, D])
    x_pre = din("x_pre", [PRE, D])
    w_in = {n: din(n, [D, D]) for n in ("wq", "wk", "wv", "wg", "wo")}
    gate_w = din("gate_w", [E, D])
    gate_b = din("gate_b", [1, E])
    We1, We3 = din("We1", [E, D, INTER]), din("We3", [E, D, INTER])
    We2 = din("We2", [E, INTER, D])
    be1, be3 = din("be1", [E, INTER]), din("be3", [E, INTER])
    be2 = din("be2", [E, D])
    ws1, ws2, ws3 = din("ws1", [D, D]), din("ws2", [D, D]), din("ws3", [D, D])
    bs1, bs2, bs3 = din("bs1", [1, D]), din("bs2", [1, D]), din("bs3", [1, D])
    ident_in = din("ident_in", [128, 128], BF16)
    cos_own = din("cos_own", [128, NT, DH])
    sin_own = din("sin_own", [128, NT, DH])
    cos_pre_h = din("cos_pre", [128, NPT, DH], BF16)
    sin_pre_h = din("sin_pre", [128, NPT, DH], BF16)
    decayT_h = din("decayT", [128, H, 128], BF16)
    gpow_h = din("gpow_k", [128, D], BF16)
    g128_h = din("g128", [64, H * DH])
    gbcast_h = din("gbcast", [64, H, 128], BF16)
    onehot_h = din("onehot", [16, 16 * 128], BF16)

    out_t = nc.dram_tensor("out", [TCORE, D], F32, kind="ExternalOutput").ap()
    xscr = nc.dram_tensor("xscr", [TCORE, D], F32, kind="Internal").ap()
    wbf = {n: nc.dram_tensor(f"{n}_bf", [D, D], BF16, kind="Internal").ap()
           for n in ("wq", "wk", "wv", "wg", "wo")}
    gwbf = nc.dram_tensor("gw_bf", [E, D], BF16, kind="Internal").ap()

    A = Arena(nc)
    sch = Sched()

    ident = A.alloc("ident", [128, 128], BF16)
    ident_f = A.alloc("ident_f", [128, 128], F32)
    R_all = A.alloc("R_all", [64, H, DH], F32)
    R_bf = A.alloc("R_bf", [64, H, DH], BF16)
    g128_sb = A.alloc("g128_sb", [64, H * DH], F32)
    gbcast_sb = A.alloc("gbcast_sb", [64, H, 128], BF16)
    gpow_sb = A.alloc("gpow_sb", [128, D], BF16)
    decay_sb = A.alloc("decay_sb", [128, H, 128], BF16)
    combine_fm = A.alloc("combine_fm", [16, TCORE], BF16)
    onehot_sb = A.alloc("onehot_sb", [16, 16 * 128], BF16)
    p_all = A.alloc("p_all", [128, NT, E], F32)
    ones_bf = A.alloc("ones_bf", [1, 128], BF16)
    ones_f = A.alloc("ones_f", [1, 128], F32)
    gateb_sb = A.alloc("gateb_sb", [1, E], F32)
    gb_bcast = A.alloc("gb_bcast", [128, E], F32)
    stat = A.alloc("stat", [128, 64], F32)
    bncols = A.alloc("bncols", [128, 8, 12], F32)
    mnt = A.alloc("mnt", [128, 8, 4], F32)
    biasT = A.alloc("biasT", [128, 4, E], F32)
    bias3T = A.alloc("bias3T", [128, 4, E], F32)
    bs1T = A.alloc("bs1T", [128, 8], F32)
    bs3T = A.alloc("bs3T", [128, 8], F32)
    be2_sb = A.alloc("be2_sb", [16, D], BF16)
    bs2_sb = A.alloc("bs2_sb", [1, D], BF16)
    Xln2_fm = A.alloc("Xln2_fm", [128, 8, TCORE], BF16)
    gsc = A.alloc("gsc", [128, 48], F32)
    s2t = A.alloc("s2t", [128, E], F32)
    comb_t = A.alloc("comb_t", [128, E], BF16)
    epscol = A.alloc("epscol", [128, 1], F32)
    persist_hi = A.off

    attn_base = A.off
    wT = {n: A.alloc(f"{n}T", [128, 8, D], BF16) for n in ("wq", "wk", "wv", "wg", "wo")}
    gateT = A.alloc("gateT", [128, 8, E], BF16)
    cos_own_sb = A.alloc("cos_own_sb", [128, NT, DH], F32)
    sin_own_sb = A.alloc("sin_own_sb", [128, NT, DH], F32)
    cos_pre_sb = A.alloc("cos_pre_sb", [128, 16, DH], BF16)
    sin_pre_sb = A.alloc("sin_pre_sb", [128, 16, DH], BF16)
    xtile = A.alloc("xtile", [128, 2, D], F32)
    ropeA = A.alloc("ropeA", [128, D], F32)
    ropeB = A.alloc("ropeB", [128, D], F32)
    xln_bf = A.alloc("xln_bf", [128, 2, D], BF16)
    xlnfm = A.alloc("xlnfm", [128, 1, D], BF16)
    q_tm = A.alloc("q_tm", [128, D], BF16)
    k_tm = A.alloc("k_tm", [128, D], BF16)
    v_tm = A.alloc("v_tm", [128, D], BF16)
    ks_bf = A.alloc("ks_bf", [128, D], BF16)
    qfm = A.alloc("qfm", [64, H, 128], BF16)
    kfm = A.alloc("kfm", [64, H, 128], BF16)
    qs_bf = A.alloc("qs_bf", [64, H, 128], BF16)
    g_tm = A.alloc("g_tm", [128, D], BF16)
    m_tm = A.alloc("m_tm", [128, D], BF16)
    mfm = q_tm
    masked = nc.alloc_sbuf_tensor_at("masked", [128, 2, 512], BF16, offset=nc.lookup_mloc(ropeA).addr)
    ln_scr = ropeB
    attn_hi = A.off

    A.off = attn_base
    y_sb = A.alloc("y_sb", [128, 8, D], F32)
    h4 = A.alloc("h4", [128, 4, 4, 1024], BF16)
    wr1 = A.alloc("wr1", [128, 2, 8, INTER], BF16)
    wr3 = A.alloc("wr3", [128, 2, 8, INTER], BF16)
    wr2 = A.alloc("wr2", [128, 4, 4, D], BF16)
    h1s = A.alloc("h1s", [128, 1024], BF16)
    biasrow = nc.alloc_sbuf_tensor_at("biasrow", [16, INTER], F32, offset=nc.lookup_mloc(h1s).addr)
    hcm = A.alloc("hcm", [128, 1024], BF16)
    cb_bf = A.alloc("cb_bf", [128, 1024], BF16)
    xres = A.alloc("xres", [128, 1, D], F32)
    ffn_hi = A.off
    assert max(attn_hi, ffn_hi) <= SBUF_END, (attn_hi, ffn_hi)

    pT = nc.alloc_psum_tensor("pT", [128, 8, 128], BF16)
    pA = nc.alloc_psum_tensor("pA", [128, 512], F32)
    pB = nc.alloc_psum_tensor("pB", [128, 512], F32)
    pQT = nc.alloc_psum_tensor("pQT", [128, 8, 128], BF16)
    pKT = nc.alloc_psum_tensor("pKT", [128, 8, 128], BF16)
    pS0 = nc.alloc_psum_tensor("pS0", [128, 512], F32)
    pS1 = nc.alloc_psum_tensor("pS1", [128, 512], F32)
    pO = nc.alloc_psum_tensor("pO", [128, 512], F32)
    pT_f = pT.ap().rearrange("p c n -> p (c n)").bitcast(F32)
    pT_f2 = pT_f
    pQT_f = pQT.ap().rearrange("p c n -> p (c n)").bitcast(F32)
    pKT_f = pKT.ap().rearrange("p c n -> p (c n)").bitcast(F32)
    YB = [[(pO, "bank_pO"), (pT_f, "bank_pT")],
          [(pQT_f, "bank_pQT"), (pKT_f, "bank_pKT")]]

    def dma(engine, out, in_, reads, writes, tag, **kw):
        def fn(eng, out=out, in_=in_, kw=kw):
            return eng.dma_start(out=out, in_=in_, **kw)
        return sch.add(engine, fn, reads=reads, writes=writes, dma=True, tag=tag)

    def ln_stats(src_ap, src_res, slot, tag):
        mean = stat[:, slot * 8 + 0:slot * 8 + 1]
        var = stat[:, slot * 8 + 1:slot * 8 + 2]
        stdv = stat[:, slot * 8 + 2:slot * 8 + 3]
        rstd = stat[:, slot * 8 + 3:slot * 8 + 4]
        nmr = stat[:, slot * 8 + 4:slot * 8 + 5]
        tmp = stat[:, slot * 8 + 5:slot * 8 + 6]
        bnc = bncols[:, slot, :]
        sres = f"stat{slot}"
        bres = f"bnc{slot}"

        def f_bn(eng):
            eng.bn_stats(bnc[:, 0:6], src_ap[:, 0:512])
            return eng.bn_stats(bnc[:, 6:12], src_ap[:, 512:1024])
        sch.add("dve", f_bn, reads=(src_res,), writes=(bres,), tag=f"{tag}_bn", slow=True)

        msum = stat[:, slot * 8 + 6:slot * 8 + 7]
        v1c = stat[:, slot * 8 + 7:slot * 8 + 8]
        mntmp = mnt[:, slot, :]
        def f_c1(eng):
            eng.tensor_reduce(msum, bnc[:, 1::3], AX.X, ALU.add)
            eng.tensor_reduce(v1c, bnc[:, 2::3], AX.X, ALU.add)
            return eng.tensor_tensor(mntmp, bnc[:, 1::3], bnc[:, 1::3], ALU.mult)
        sch.add("dve", f_c1, reads=(bres,), writes=(sres + "a", bres + "t"), tag=f"{tag}_c1")

        def f_c2(eng):
            return eng.tensor_reduce(tmp, mntmp, AX.X, ALU.add)
        sch.add("dve", f_c2, reads=(bres + "t",), writes=(sres + "b",), tag=f"{tag}_c2")

        def f_c3(eng):
            eng.tensor_scalar_mul(mean, msum, 0.25)
            return eng.scalar_tensor_tensor(v1c, tmp, 256.0, v1c, ALU.mult, ALU.add)
        sch.add("dve", f_c3, reads=(sres + "a", sres + "b"), writes=(sres + "c",), tag=f"{tag}_c3")

        def f_c4(eng):
            return eng.tensor_tensor(tmp, mean, mean, ALU.mult)
        sch.add("dve", f_c4, reads=(sres + "c",), writes=(sres + "d",), tag=f"{tag}_c4")

        def f_c5(eng):
            return eng.scalar_tensor_tensor(var, v1c, 1.0 / D, tmp, ALU.mult, ALU.subtract)
        sch.add("dve", f_c5, reads=(sres + "c", sres + "d"), writes=(sres,), tag=f"{tag}_c5")

        def f_sqrt(eng):
            return eng.activation(stdv, var, AF.Sqrt, bias=epscol[:, :])
        sch.add("act", f_sqrt, reads=(sres, "eps"), writes=(sres + "s",), tag=f"{tag}_sq")

        def f_recip(eng):
            return eng.reciprocal(rstd, stdv)
        sch.add("dve", f_recip, reads=(sres + "s",), writes=(sres + "q",), tag=f"{tag}_rc", slow=True)

        def f_nmr(eng):
            return eng.scalar_tensor_tensor(nmr, mean, -1.0, rstd, ALU.mult, ALU.mult)
        sch.add("dve", f_nmr, reads=(sres + "q", sres), writes=(sres + "r",), tag=f"{tag}_r")
        return rstd, nmr, sres + "r"

    def ln_apply(dst_ap, dst_res, src_ap, src_res, rstd, nmr, rres, tag):
        def f(eng):
            return eng.activation(dst_ap, src_ap, AF.Identity, bias=nmr, scale=rstd)
        sch.add("act", f, reads=(src_res, rres), writes=(dst_res,), tag=f"{tag}_ap")

    def transpose8(src_sb, src_res, dst_ap, dst_res, tag, pbank=pT, pres="bank_pT"):
        for ch in range(8):
            def tr(eng, ch=ch):
                return eng.transpose(pbank[:, ch, :], src_sb[:, ch * 128:(ch + 1) * 128], ident[:, :])
            sch.add("pe", tr, reads=(src_res, "ident"), writes=(pres,), tag=f"{tag}_t{ch}")

        def ev(eng):
            d3 = dst_ap if dst_ap.ndim == 3 else dst_ap.rearrange("p (c n) -> p c n", n=128)
            return eng.tensor_copy(d3, pbank[:, :, :])
        sch.add("dve", ev, reads=(pres,), writes=(dst_res, pres), tag=f"{tag}_ev")

    PAIRS = [((pA, "bank_pA"), (pB, "bank_pB")), ((pS0, "bank_pS0"), (pS1, "bank_pS1"))]

    def proj(wname, xf_ap, xf_res, tag, pair=0):
        (b0, r0), (b1, r1) = PAIRS[pair]
        for ch in range(8):
            for half, pb, pres in ((0, b0, r0), (1, b1, r1)):
                def mm(eng, ch=ch, half=half, pb=pb, wname=wname, xf_ap=xf_ap):
                    return eng.matmul(pb[:, :], xf_ap[:, ch * 128:(ch + 1) * 128],
                                      wT[wname][:, ch, half * 512:(half + 1) * 512],
                                      start=(ch == 0), stop=(ch == 7))
                sch.add("pe", mm, reads=(f"{wname}T", xf_res), writes=(pres,), tag=f"{tag}{half}_{ch}")

    def rope(dst_ap, dst_res, cos_sb, sin_sb, cres, ti, tag, pair=0):
        (pa_t, r0), (pb_t, r1) = PAIRS[pair]
        c8 = cos_sb[:, ti, :][:, None, :].broadcast_to([128, 8, DH])
        s0 = sin_sb[:, ti, 0:32][:, None, :].broadcast_to([128, 8, 32])
        s1 = sin_sb[:, ti, 32:64][:, None, :].broadcast_to([128, 8, 32])

        def f1(eng):
            eng.tensor_tensor(ropeA.ap().rearrange("p (h d) -> p h d", d=DH)[:, 0:8, :],
                              pa_t.ap().rearrange("p (h d) -> p h d", d=DH), c8, ALU.mult)
            return eng.tensor_tensor(ropeA.ap().rearrange("p (h d) -> p h d", d=DH)[:, 8:16, :],
                                     pb_t.ap().rearrange("p (h d) -> p h d", d=DH), c8, ALU.mult)
        sch.add("dve", f1, reads=(r0, r1, cres), writes=("ropeA",), tag=f"{tag}_c")

        def f2(eng):
            rv = ropeB.ap().rearrange("p (h s d) -> p h s d", s=2, d=32)
            pa = pa_t.ap().rearrange("p (h s d) -> p h s d", s=2, d=32)
            pb = pb_t.ap().rearrange("p (h s d) -> p h s d", s=2, d=32)
            eng.tensor_tensor(rv[:, 0:8, 0, :], pa[:, :, 1, :], s0, ALU.mult)
            eng.tensor_tensor(rv[:, 0:8, 1, :], pa[:, :, 0, :], s1, ALU.mult)
            eng.tensor_tensor(rv[:, 8:16, 0, :], pb[:, :, 1, :], s0, ALU.mult)
            return eng.tensor_tensor(rv[:, 8:16, 1, :], pb[:, :, 0, :], s1, ALU.mult)
        sch.add("dve", f2, reads=(r0, r1, cres), writes=("ropeB",), tag=f"{tag}_s")

        def f3(eng):
            return eng.tensor_add(dst_ap, ropeA[:, :], ropeB[:, :])
        sch.add("dve", f3, reads=("ropeA", "ropeB"), writes=(dst_res, r0, r1), tag=f"{tag}_a")

    def state_update(vsrc_ap, vsrc_res, ksrc_ap, ksrc_res, tag):
        def ksf(eng):
            return eng.tensor_tensor(ks_bf[:, :], ksrc_ap, gpow_sb[:, :], ALU.mult)
        sch.add("dve", ksf, reads=(ksrc_res, "gpow"), writes=("ks_bf",), tag=f"{tag}_ks")

        def rup1(eng):
            return eng.tensor_tensor(R_all.ap().rearrange("p h d -> p (h d)"),
                                     R_all.ap().rearrange("p h d -> p (h d)"),
                                     g128_sb[:, :], ALU.mult)
        sch.add("dve", rup1, reads=("g128", "R_all"), writes=("R_all",), tag=f"{tag}_rup1")

        for rnd in range(2):
            for hh in range(8):
                h = rnd * 8 + hh

                def smm(eng, hh=hh, h=h):
                    return eng.matmul(pT_f[0:64, hh * 64:(hh + 1) * 64],
                                      ks_bf[:, h * DH:(h + 1) * DH], vsrc_ap[:, h * DH:(h + 1) * DH],
                                      start=True, stop=True)
                sch.add("pe", smm, reads=("ks_bf", vsrc_res), writes=("bank_pT",), tag=f"{tag}_smm{h}")

            def rup2(eng, rnd=rnd):
                return eng.tensor_tensor(
                    R_all[:, rnd * 8:(rnd + 1) * 8, :].rearrange("p h d -> p (h d)"),
                    R_all[:, rnd * 8:(rnd + 1) * 8, :].rearrange("p h d -> p (h d)"),
                    pT_f[0:64, :], ALU.add)
            sch.add("dve", rup2, reads=("bank_pT", "R_all"), writes=("R_all", "bank_pT"), tag=f"{tag}_rup2{rnd}")

        def rbf(eng):
            return eng.tensor_copy(R_bf[:, :, :], R_all[:, :, :])
        sch.add("dve", rbf, reads=("R_all",), writes=("R_bf",), tag=f"{tag}_rbf")

    dma("sp", ident[:, :], ident_in[:, :], (), ("ident",), "ld_ident")
    dma("sp", cos_own_sb[:, :, :], cos_own[:, :, :], (), ("cos_own",), "ld_cos")
    dma("sp", sin_own_sb[:, :, :], sin_own[:, :, :], (), ("cos_own",), "ld_sin")
    dma("sp", decay_sb[:, :, :], decayT_h[:, :, :], (), ("decay",), "ld_decay")
    dma("sp", gpow_sb[:, :], gpow_h[:, :], (), ("gpow",), "ld_gpow")
    dma("sp", g128_sb[:, :], g128_h[:, :], (), ("g128",), "ld_g128")
    dma("sp", gbcast_sb[:, :, :], gbcast_h[:, :, :], (), ("gbcast",), "ld_gb")
    dma("sp", gateb_sb[:, :], gate_b[:, :], (), ("gateb",), "ld_gateb")
    dma("sp", onehot_sb[:, :], onehot_h[:, :], (), ("onehot",), "ld_oh")
    sch.add("dve", lambda eng: eng.memset(epscol[:, :], EPS), writes=("eps",), tag="eps")
    sch.add("dve", lambda eng: eng.memset(ones_bf[:, :], 1.0), writes=("ones",), tag="ones")
    sch.add("dve", lambda eng: eng.memset(ones_f[:, :], 1.0), writes=("ones_f",), tag="ones_f")
    sch.add("dve", lambda eng: eng.memset(R_all[:, :, :], 0.0), writes=("R_all",), tag="R0")
    sch.add("dve", lambda eng: eng.memset(R_bf[:, :, :], 0.0), writes=("R_bf",), tag="R0b")
    sch.add("dve", lambda eng: eng.tensor_copy(ident_f[:, :], ident[:, :]),
            reads=("ident",), writes=("identf",), tag="identf")

    for n in ("wq", "wk", "wv", "wg", "wo"):
        dma("pool", wbf[n][:, :], w_in[n][:, :], (), (f"{n}bf",), f"cast_{n}")
        for ch in range(8):
            dma("sp", wT[n][:, ch, :], wbf[n][0:D, ch * 128:(ch + 1) * 128],
                (f"{n}bf",), (f"{n}T",), f"tr_{n}{ch}", transpose=True)
    dma("pool", gwbf[:, :], gate_w[:, :], (), ("gwbf",), "cast_gw")
    for ch in range(8):
        dma("sp", gateT[:, ch, :], gwbf[0:E, ch * 128:(ch + 1) * 128],
            ("gwbf",), ("gateT",), f"tr_gw{ch}", transpose=True)
    dma("pool", be2_sb[:, :], be2[:, :], (), ("be2",), "cast_be2")
    dma("pool", bs2_sb[:, :], bs2[:, :], (), ("bs2",), "cast_bs2")

    def bias_transpose(name, hbm_view, rows, cols, dst_fn, res):
        dma("sp", biasrow[0:rows, 0:cols], hbm_view, ("brow",), ("brow",), f"ld_{name}")
        for j in range(cols // 128):
            def tr(eng, j=j, rows=rows):
                return eng.transpose(pA[0:128, 0:rows], biasrow[0:rows, j * 128:(j + 1) * 128], ident_f[0:rows, 0:rows])
            sch.add("pe", tr, reads=("brow", "identf"), writes=("bank_pA",), tag=f"trb_{name}{j}")

            def ev(eng, j=j, rows=rows):
                return eng.tensor_copy(dst_fn(j, rows), pA[0:128, 0:rows])
            sch.add("dve", ev, reads=("bank_pA",), writes=(res, "bank_pA"), tag=f"evb_{name}{j}")

    bias_transpose("be1", be1[:, :], E, INTER, lambda j, r: biasT[:, j, 0:r], "be1T")
    bias_transpose("be3", be3[:, :], E, INTER, lambda j, r: bias3T[:, j, 0:r], "be3T")
    bias_transpose("bs1", bs1.rearrange("o (c p) -> (o c) p", p=128), 8, 128,
                   lambda j, r: bs1T[:, 0:r], "bs1T")
    bias_transpose("bs3", bs3.rearrange("o (c p) -> (o c) p", p=128), 8, 128,
                   lambda j, r: bs3T[:, 0:r], "bs3T")

    def gbb(eng):
        return eng.matmul(pA[:, 0:E], ones_f[0:1, :], gateb_sb[0:1, :], start=True, stop=True)
    sch.add("pe", gbb, reads=("ones_f", "gateb"), writes=("bank_pA",), tag="gbb")
    sch.add("dve", lambda eng: eng.tensor_copy(gb_bcast[:, :], pA[:, 0:E]),
            reads=("bank_pA",), writes=("gbb_r", "bank_pA"), tag="gbb_ev")

    PRE_ON = STAGE not in ("ln1", "qproj", "own2")
    def pre_A(pt):
        buf = pt % 2
        xa = xtile[:, buf, :]
        xr = f"xtile{buf}"
        dma("sp", xa, x_pre.rearrange("(t p) d -> p t d", p=128)[:, pt, :],
            (xr,), (xr,), f"pre_ld{pt}")
        return ln_stats(xa, xr, buf, f"pre{pt}")

    def pre_B(pt, st):
        buf = pt % 2
        ti = pt % 16
        xa = xtile[:, buf, :]
        xr = f"xtile{buf}"
        rstd, nmr, rres = st
        ln_apply(xln_bf[:, buf, :], f"xln{buf}", xa, xr, rstd, nmr, rres, f"pre{pt}")
        transpose8(xln_bf[:, buf, :], f"xln{buf}", xlnfm[:, 0, :], "xlnfm0", f"pre{pt}")
        proj("wk", xlnfm[:, 0, :], "xlnfm0", f"pre{pt}k", pair=0)
        rope(k_tm[:, :], "k_tm", cos_pre_sb, sin_pre_sb, "cos_pre", ti, f"pre{pt}kr", pair=0)
        proj("wv", xlnfm[:, 0, :], "xlnfm0", f"pre{pt}v", pair=1)

        def vev(eng):
            eng.tensor_copy(v_tm[:, 0:512], pS0[:, :])
            return eng.tensor_copy(v_tm[:, 512:1024], pS1[:, :])
        sch.add("dve", vev, reads=("bank_pS0", "bank_pS1"),
                writes=("v_tm", "bank_pS0", "bank_pS1"), tag=f"pre{pt}_vev")
        state_update(v_tm[:, :], "v_tm", k_tm[:, :], "k_tm", f"pre{pt}")

    if PRE_ON:
        dma("sp", cos_pre_sb[:, :, :], cos_pre_h[:, 0:16, :], ("cos_pre",), ("cos_pre",), "ld_cosp0")
        dma("sp", sin_pre_sb[:, :, :], sin_pre_h[:, 0:16, :], ("cos_pre",), ("cos_pre",), "ld_sinp0")
        st = pre_A(0)
        for pt in range(NPT):
            stn = pre_A(pt + 1) if pt + 1 < NPT else None
            pre_B(pt, st)
            if pt + 1 < NPT and (pt + 1) % 16 == 0:
                ch = (pt + 1) // 16
                dma("sp", cos_pre_sb[:, :, :], cos_pre_h[:, ch * 16:(ch + 1) * 16, :],
                    ("cos_pre",), ("cos_pre",), f"ld_cosp{ch}")
                dma("sp", sin_pre_sb[:, :, :], sin_pre_h[:, ch * 16:(ch + 1) * 16, :],
                    ("cos_pre",), ("cos_pre",), f"ld_sinp{ch}")
            st = stn

    if STAGE == "pre":
        def rdump(eng):
            return eng.tensor_copy(xtile[0:64, 0, 0:1024], R_all.ap().rearrange("p h d -> p (h d)"))
        sch.add("dve", rdump, reads=("R_all",), writes=("xtile0",), tag="rdump")
    if STAGE == "pre2":
        def rdump2(eng):
            return eng.tensor_copy(xtile[0:64, 0, 0:1024], R_bf.ap().rearrange("p h d -> p (h d)"))
        sch.add("dve", rdump2, reads=("R_bf",), writes=("xtile0",), tag="rdump2")
        dma("sp", out_t.rearrange("(t p) d -> p t d", p=128)[:, 0, :],
            xtile[:, 0, :], ("xtile0",), ("out_done",), "dbgR2")
        dma("sp", out_t.rearrange("(t p) d -> p t d", p=128)[:, 0, :],
            xtile[:, 0, :], ("xtile0",), ("out_done",), "dbgR")

    def emit_xln2_T(t):
        transpose8(ks_bf[:, :], "ks_bf",
                   Xln2_fm[:, :, t * 128:(t + 1) * 128],
                   f"Xln2_{t}", f"o{t}x2T")

    def own_A(t):
        buf = t % 2
        xa = xtile[:, buf, :]
        xr = f"xtile{buf}"
        dma("sp", xa, x_own.rearrange("(t p) d -> p t d", p=128)[:, t, :],
            (xr,), (xr,), f"own_ld{t}")
        return ln_stats(xa, xr, buf, f"o{t}")

    def own_B(t, st):
        buf = t % 2
        xa = xtile[:, buf, :]
        xr = f"xtile{buf}"
        rstd, nmr, rres = st
        ln_apply(xln_bf[:, buf, :], f"xln{buf}", xa, xr, rstd, nmr, rres, f"o{t}")
        if STAGE == "ln1":
            dma("pool", out_t.rearrange("(t p) d -> p t d", p=128)[:, t, :],
                xln_bf[:, buf, :], (f"xln{buf}",), (f"xln{buf}", "out_done"), f"dbg{t}")
            return
        if STAGE == "ln1dbg":
            dma("sp", out_t.rearrange("(t p) d -> p t d", p=128)[:, t, 0:8],
                stat[:, buf * 8:buf * 8 + 8], (f"stat{buf}r",), ("out_done",), f"dbgs{t}")
            dma("sp", out_t.rearrange("(t p) d -> p t d", p=128)[:, t, 8:20],
                bncols[:, buf, :], (f"bnc{buf}",), ("out_done",), f"dbgb{t}")
            dma("sp", out_t.rearrange("(t p) d -> p t d", p=128)[:, t, 32:1024],
                xtile[:, buf, 32:1024], (xr,), ("out_done",), f"dbgx{t}")
            return
        transpose8(xln_bf[:, buf, :], f"xln{buf}", xlnfm[:, 0, :], "xlnfm0", f"o{t}")
        if t == 0 and STAGE == "ret" and KRET == "dumpR":
            def rsnap(eng):
                return eng.tensor_copy(xln_bf[0:64, 0, :], R_bf.ap().rearrange("p h d -> p (h d)"))
            sch.add("dve", rsnap, reads=("R_bf",), writes=("xln0",), tag="rsnap")
            dma("pool", out_t.rearrange("(t p) d -> p t d", p=128)[:, 15, :],
                xln_bf[:, 0, :], ("xln0",), ("xln0", "rsnapd"), "rsnapdma")
        xf, xfr = xlnfm[:, 0, :], "xlnfm0"

        proj("wq", xf, xfr, f"o{t}q")
        rope(q_tm[:, :], "q_tm", cos_own_sb, sin_own_sb, "cos_own", t, f"o{t}qr")
        if STAGE == "qproj":
            dma("pool", out_t.rearrange("(t p) d -> p t d", p=128)[:, t, :],
                q_tm[:, :], ("q_tm",), ("q_tm", "out_done"), f"dbg{t}")
            return
        proj("wk", xf, xfr, f"o{t}k")
        rope(k_tm[:, :], "k_tm", cos_own_sb, sin_own_sb, "cos_own", t, f"o{t}kr")
        proj("wv", xf, xfr, f"o{t}v")

        def vev(eng):
            eng.tensor_copy(v_tm[:, 0:512], pA[:, :])
            return eng.tensor_copy(v_tm[:, 512:1024], pB[:, :])
        sch.add("dve", vev, reads=("bank_pA", "bank_pB"),
                writes=("v_tm", "bank_pA", "bank_pB"), tag=f"o{t}_vev")

        proj("wg", xf, xfr, f"o{t}g")

        def gev(eng):
            eng.activation(g_tm[:, 0:512], pA[:, :], AF.Silu)
            return eng.activation(g_tm[:, 512:1024], pB[:, :], AF.Silu)
        sch.add("act", gev, reads=("bank_pA", "bank_pB"),
                writes=("g_tm", "bank_pA", "bank_pB"), tag=f"o{t}_gev")

        for rnd in range(2):
            for hh in range(8):
                h = rnd * 8 + hh

                def qtr(eng, hh=hh, h=h):
                    return eng.transpose(pQT[0:64, hh, :], q_tm[:, h * DH:(h + 1) * DH], ident[:, :])
                sch.add("pe", qtr, reads=("q_tm", "ident"), writes=("bank_pQT",), tag=f"o{t}_qT{h}")

                def ktr(eng, hh=hh, h=h):
                    return eng.transpose(pKT[0:64, hh, :], k_tm[:, h * DH:(h + 1) * DH], ident[:, :])
                sch.add("pe", ktr, reads=("k_tm", "ident"), writes=("bank_pKT",), tag=f"o{t}_kT{h}")

            def qev(eng, rnd=rnd):
                return eng.tensor_copy(qfm[:, rnd * 8:(rnd + 1) * 8, :], pQT[0:64, :, :])
            sch.add("dve", qev, reads=("bank_pQT",), writes=("qfm", "bank_pQT"), tag=f"o{t}_qev{rnd}")

            def kev(eng, rnd=rnd):
                return eng.tensor_copy(kfm[:, rnd * 8:(rnd + 1) * 8, :], pKT[0:64, :, :])
            sch.add("dve", kev, reads=("bank_pKT",), writes=("kfm", "bank_pKT"), tag=f"o{t}_kev{rnd}")

        if STAGE == "own2":
            dma("pool", out_t.rearrange("(t p) d -> p t d", p=128)[:, t, 0:512],
                qfm.ap().rearrange("p h n -> p (h n)")[:, 0:512], ("qfm",), ("qfm", "out_done"), f"dbg{t}")
            return

        def qsf(eng):
            return eng.tensor_tensor(qs_bf[:, :, :], qfm[:, :, :], gbcast_sb[:, :, :], ALU.mult)
        sch.add("dve", qsf, reads=("qfm", "gbcast"), writes=("qs_bf",), tag=f"o{t}_qs")

        for g4 in range(4 if KRET != "a0" else 0):
            ps, pres = (pS0, "bank_pS0") if g4 % 2 == 0 else (pS1, "bank_pS1")
            psv = ps.ap().rearrange("p (c n) -> p c n", n=128)
            for hh in range(4):
                h = g4 * 4 + hh
                cp, par = h // 2, h % 2

                def scmm(eng, h=h, hh=hh, psv=psv):
                    return eng.matmul(psv[:, hh, :],
                                      kfm[0:64, h, :], qfm[0:64, h, :],
                                      start=True, stop=True)
                sch.add("pe", scmm, reads=("kfm", "qfm"), writes=(pres,), tag=f"o{t}_sc{h}")

            def mskf(eng, g4=g4, ps=ps):
                return eng.tensor_tensor(masked[:, g4 % 2, :], ps[:, :],
                                         decay_sb[:, 4 * g4:4 * (g4 + 1), :].rearrange("p c n -> p (c n)"),
                                         ALU.mult)
            sch.add("dve", mskf, reads=(pres, "decay"), writes=("ropeA", pres), tag=f"o{t}_msk{g4}")

            for hh in range(4 if KRET not in ("a",) else 0):
                h = g4 * 4 + hh
                cp, par = h // 2, h % 2

                def omm1(eng, g4=g4, hh=hh, h=h):
                    return eng.matmul(pO[:, (h % 8) * 64:(h % 8 + 1) * 64],
                                      masked[:, g4 % 2, hh * 128:(hh + 1) * 128],
                                      v_tm[:, h * DH:(h + 1) * DH], start=True, stop=False)
                if KRET == "b":
                    def omm1b(eng, g4=g4, hh=hh, h=h):
                        return eng.matmul(pO[:, (h % 8) * 64:(h % 8 + 1) * 64],
                                          masked[:, g4 % 2, hh * 128:(hh + 1) * 128],
                                          v_tm[:, h * DH:(h + 1) * DH], start=True, stop=True)
                    sch.add("pe", omm1b, reads=("ropeA", "v_tm"), writes=("bank_pO",), tag=f"o{t}_oi{h}")
                else:
                    sch.add("pe", omm1, reads=("ropeA", "v_tm"), writes=("bank_pO",), tag=f"o{t}_oi{h}")

                    def omm2(eng, h=h):
                        return eng.matmul(pO[:, (h % 8) * 64:(h % 8 + 1) * 64],
                                          qs_bf[0:64, h, :], R_bf[0:64, h, :],
                                          start=False, stop=True)
                    sch.add("pe", omm2, reads=("qs_bf", "R_bf"), writes=("bank_pO",), tag=f"o{t}_ox{h}")

            if g4 % 2 == 1 and KRET not in ("a",):
                half = g4 // 2

                def mf(eng, half=half):
                    return eng.tensor_tensor(m_tm[:, half * 512:(half + 1) * 512], pO[:, :],
                                             g_tm[:, half * 512:(half + 1) * 512], ALU.mult)
                sch.add("dve", mf, reads=("bank_pO", "g_tm"), writes=("m_tm", "bank_pO"), tag=f"o{t}_m{half}")

        if KRET == "d":
            state_update(v_tm[:, :], "v_tm", k_tm[:, :], "k_tm", f"o{t}")
        if STAGE == "ret":
            if t == 0 and KRET == "dumpR":
                def rdump3(eng):
                    return eng.tensor_copy(m_tm[0:64, 0:1024], R_bf.ap().rearrange("p h d -> p (h d)"))
                sch.add("dve", rdump3, reads=("R_bf",), writes=("m_tm",), tag="rdump3")
            if KRET == "a":
                dma("pool", out_t.rearrange("(t p) d -> p t d", p=128)[:, t, :],
                    masked.ap().rearrange("p c n -> p (c n)"), ("ropeA",),
                    ("ropeA", "out_done"), f"dbg{t}")
            else:
                dma("pool", out_t.rearrange("(t p) d -> p t d", p=128)[:, t, :],
                    m_tm[:, :], ("m_tm",), ("m_tm", "out_done"), f"dbg{t}")
            return

        transpose8(m_tm[:, :], "m_tm", mfm[:, :], "q_tm", f"o{t}mT")
        proj("wo", mfm[:, :], "q_tm", f"o{t}wo")

        def aev(eng):
            eng.activation(ln_scr[:, 0:512], pA[:, :], AF.Identity)
            return eng.activation(ln_scr[:, 512:1024], pB[:, :], AF.Identity)
        sch.add("act", aev, reads=("bank_pA", "bank_pB"),
                writes=("ropeB", "bank_pA", "bank_pB"), tag=f"o{t}_aev")
        rstd2, nmr2, rres2 = ln_stats(ln_scr[:, :], "ropeB", 2 + buf, f"o{t}l2")

        def f_ap2(eng, rstd2=rstd2, nmr2=nmr2):
            return eng.activation(ropeA[:, :], ln_scr[:, :], AF.Identity, bias=nmr2, scale=rstd2)
        sch.add("act", f_ap2, reads=("ropeB", rres2), writes=("ropeA",), tag=f"o{t}_ap2")

        def f_add2(eng, xa=xa):
            return eng.tensor_add(xa, xa, ropeA[:, :])
        sch.add("dve", f_add2, reads=("ropeA", xr), writes=(xr,), tag=f"o{t}_res")

        rstd3, nmr3, rres3 = ln_stats(xa, xr, 4 + buf, f"o{t}l3")
        ln_apply(xln_bf[:, buf, :], f"xln{buf}", xa, xr, rstd3, nmr3, rres3, f"o{t}l3")
        transpose8(xln_bf[:, buf, :], f"xln{buf}",
                   Xln2_fm[:, :, t * 128:(t + 1) * 128],
                   f"Xln2_{t}", f"o{t}x2T")
        dma("sp", xscr.rearrange("(t p) d -> p t d", p=128)[:, t, :], xa,
            (xr,), (xr, f"xscr_{t}"), f"spill{t}")


    if STAGE not in ("pre", "pre2"):
        nt_run = min(NT, KOWNT)
        if nt_run:
            st = own_A(0)
            for t in range(nt_run):
                stn = own_A(t + 1) if t + 1 < nt_run else None
                own_B(t, st)
                st = stn
    sch.add("dve", lambda eng: eng.engine_nop(),
            reads=tuple(f"Xln2_{t}" for t in range(NT)), writes=("Xln2_all",), tag="x2all")

    if STAGE == "attn":
        for t in range(NT):
            dma("sp", xtile[:, 0, :], xscr.rearrange("(t p) d -> p t d", p=128)[:, t, :],
                ("xtile0", f"xscr_{t}"), ("xtile0",), f"dbgld{t}")
            dma("sp", out_t.rearrange("(t p) d -> p t d", p=128)[:, t, :],
                xtile[:, 0, :], ("xtile0",), ("xtile0", "out_done"), f"dbgst{t}")
    for t in (range(NT) if STAGE == "full" else ()):
        for ch in range(8):
            def mm(eng, ch=ch, t=t):
                return eng.matmul(pA[:, 0:E], Xln2_fm[:, ch, t * 128:(t + 1) * 128],
                                  gateT[:, ch, :], start=(ch == 0), stop=(ch == 7))
            sch.add("pe", mm, reads=(f"Xln2_{t}", "gateT"), writes=("bank_pA",), tag=f"g{t}_{ch}")

        nmx, sm, rs = gsc[:, 40:41], gsc[:, 41:42], gsc[:, 42:43]

        def f_mx(eng):
            return eng.reduce_max(nmx, pA[:, 0:E], axis=AX.X, negate=True)
        sch.add("dve", f_mx, reads=("bank_pA",), writes=("gmx",), tag=f"g{t}_mx", slow=True)

        def f_exp(eng):
            return eng.activation(s2t[:, :], pA[:, 0:E], AF.Exp, bias=nmx, accum_out=sm)
        sch.add("act", f_exp, reads=("bank_pA", "gmx"), writes=("gexp", "bank_pA"), tag=f"g{t}_exp")

        def f_rp(eng):
            return eng.reciprocal(rs, sm)
        sch.add("dve", f_rp, reads=("gexp",), writes=("grs",), tag=f"g{t}_rp", slow=True)

        def f_p(eng, t=t):
            return eng.tensor_scalar(p_all[:, t, :], s2t[:, :], rs, None, ALU.mult)
        sch.add("dve", f_p, reads=("gexp", "grs"), writes=("p_all", "gexp"), tag=f"g{t}_p")

        def f_r0(eng, t=t):
            return eng.tensor_add(s2t[:, :], p_all[:, t, :], gb_bcast[:, :])
        sch.add("dve", f_r0, reads=("p_all", "gbb_r"), writes=("grt0",), tag=f"g{t}_r0")

        def f_r1(eng):
            s2v = s2t.ap().rearrange("p (g e) -> p g e", g=4)
            m1v = gsc[:, 0:4].rearrange("p (g o) -> p g o", o=1)
            return eng.tensor_reduce(m1v, s2v, AX.X, ALU.max)
        sch.add("dve", f_r1, reads=("grt0",), writes=("grt1",), tag=f"g{t}_r1")

        def f_r2a(eng):
            s2v = s2t.ap().rearrange("p (g e) -> p g e", g=4)
            m1v = gsc[:, 0:4].rearrange("p (g o) -> p g o", o=1)
            eqt = gsc[:, 16:32].rearrange("p (g e) -> p g e", g=4)
            return eng.tensor_tensor(eqt, s2v, m1v.broadcast_to([128, 4, 4]), ALU.is_ge)
        sch.add("dve", f_r2a, reads=("grt1", "grt0"), writes=("grt2a",), tag=f"g{t}_r2a")

        def f_r2b(eng):
            s2v = s2t.ap().rearrange("p (g e) -> p g e", g=4)
            eqt = gsc[:, 16:32].rearrange("p (g e) -> p g e", g=4)
            return eng.scalar_tensor_tensor(eqt, eqt, -1e30, s2v, ALU.mult, ALU.add)
        sch.add("dve", f_r2b, reads=("grt2a", "grt0"), writes=("grt2b",), tag=f"g{t}_r2b")

        def f_r2c(eng):
            eqt = gsc[:, 16:32].rearrange("p (g e) -> p g e", g=4)
            m2v = gsc[:, 4:8].rearrange("p (g o) -> p g o", o=1)
            return eng.tensor_reduce(m2v, eqt, AX.X, ALU.max)
        sch.add("dve", f_r2c, reads=("grt2b",), writes=("grt2",), tag=f"g{t}_r2c")

        def f_r3a(eng):
            return eng.tensor_add(gsc[:, 8:12], gsc[:, 0:4], gsc[:, 4:8])
        sch.add("dve", f_r3a, reads=("grt2", "grt1"), writes=("grt3a",), tag=f"g{t}_r3a")

        def f_r3b(eng):
            return eng.tensor_reduce(gsc[:, 12:13], gsc[:, 8:12], AX.X, ALU.max)
        sch.add("dve", f_r3b, reads=("grt3a",), writes=("grt3",), tag=f"g{t}_r3b")

        def f_r4a(eng):
            return eng.tensor_scalar(gsc[:, 40:44], gsc[:, 8:12], gsc[:, 12:13], None, ALU.is_ge)
        sch.add("dve", f_r4a, reads=("grt3", "grt3a"), writes=("grt4a",), tag=f"g{t}_r4a")

        def f_r4b(eng):
            return eng.scalar_tensor_tensor(gsc[:, 40:44], gsc[:, 40:44], -1e30, gsc[:, 8:12], ALU.mult, ALU.add)
        sch.add("dve", f_r4b, reads=("grt4a", "grt3a"), writes=("grt4b",), tag=f"g{t}_r4b")

        def f_r4c(eng):
            return eng.tensor_reduce(gsc[:, 13:14], gsc[:, 40:44], AX.X, ALU.max)
        sch.add("dve", f_r4c, reads=("grt4b",), writes=("grt4",), tag=f"g{t}_r4c")

        def f_r5a(eng):
            return eng.tensor_scalar(gsc[:, 0:4], gsc[:, 8:12], gsc[:, 13:14], None, ALU.is_ge)
        sch.add("dve", f_r5a, reads=("grt4", "grt3a", "grt1"), writes=("grt5a",), tag=f"g{t}_r5a")

        def f_r5b(eng):
            return eng.tensor_scalar(gsc[:, 0:4], gsc[:, 0:4], 1.0, 1e30, ALU.subtract, ALU.mult)
        sch.add("dve", f_r5b, reads=("grt5a",), writes=("grt5b",), tag=f"g{t}_r5b")

        def f_r5c(eng):
            s2v = s2t.ap().rearrange("p (g e) -> p g e", g=4)
            keepv = gsc[:, 0:4].rearrange("p (g o) -> p g o", o=1)
            return eng.tensor_tensor(s2v, s2v, keepv.broadcast_to([128, 4, 4]), ALU.add)
        sch.add("dve", f_r5c, reads=("grt5b", "grt2b"), writes=("grt5c",), tag=f"g{t}_r5c")

        def f_r5d(eng):
            return eng.max(gsc[:, 16:24], s2t[:, :])
        sch.add("dve", f_r5d, reads=("grt5c", "grt2a"), writes=("grt5",), tag=f"g{t}_r5d")

        def f_r6(eng):
            return eng.tensor_scalar(gsc[:, 24:40], s2t[:, :], gsc[:, 17:18], None, ALU.is_ge)
        sch.add("dve", f_r6, reads=("grt5", "grt5c"), writes=("grt6",), tag=f"g{t}_r6")

        def f_route(eng, t=t):
            return eng.tensor_tensor(comb_t[:, :], gsc[:, 24:40], p_all[:, t, :], ALU.mult)
        sch.add("dve", f_route, reads=("grt6", "p_all"), writes=("comb",), tag=f"g{t}_rt")

        def f_ct(eng):
            return eng.transpose(pT[0:16, 0, :], comb_t[:, :], ident[:, :])
        sch.add("pe", f_ct, reads=("comb", "ident"), writes=("bank_pT",), tag=f"g{t}_ct")

        def f_cev(eng, t=t):
            return eng.tensor_copy(combine_fm[:, t * 128:(t + 1) * 128], pT[0:16, 0, :])
        sch.add("dve", f_cev, reads=("bank_pT",), writes=("combine_fm", "bank_pT"), tag=f"g{t}_cev")

    sch.barrier()
    rounds = ([[0, 1, 2, 3], [4, 5, 6, 7], [8, 9, 10, 11], [12, 13, 14, 15], [16, 17]]
              if STAGE == "full" else [])
    load_seq = 0

    def ffn_weight_load(e, tag):
        nonlocal load_seq
        s1 = load_seq % 2
        s2_ = load_seq % 4
        load_seq += 1
        if e < E:
            src1 = We1.rearrange("e (c p) i -> e p c i", p=128)[e, :, :, :]
            src3 = We3.rearrange("e (c p) i -> e p c i", p=128)[e, :, :, :]
            src2 = We2.rearrange("e (c p) d -> e p c d", p=128)[e, :, :, :]
        else:
            hf = e - E
            src1 = ws1.rearrange("(c p) j -> p c j", p=128)[:, :, hf * 512:(hf + 1) * 512]
            src3 = ws3.rearrange("(c p) j -> p c j", p=128)[:, :, hf * 512:(hf + 1) * 512]
            src2 = ws2.rearrange("(c p) d -> p c d", p=128)[:, 4 * hf:4 * (hf + 1), :]
        dma("pool", wr1[:, s1, :, :], src1, (f"wr1_{s1}",), (f"wr1_{s1}",), f"{tag}w1")
        dma("pool", wr3[:, s1, :, :], src3, (f"wr3_{s1}",), (f"wr3_{s1}",), f"{tag}w3")
        dma("pool", wr2[:, s2_, :, :], src2, (f"wr2_{s2_}",), (f"wr2_{s2_}",), f"{tag}w2")
        return s1, s2_

    pS0v = pS0
    pS1v = pS1
    for gi in (range(2) if STAGE == "full" else ()):
        tok0 = gi * 1024
        for ri, rnd in enumerate(rounds):
            w2slots = []
            for ei, e in enumerate(rnd):
                s1, s2_ = ffn_weight_load(e, f"f{gi}r{ri}e{e}")
                w2slots.append(s2_)
                is_sh = e >= E

                if not is_sh:
                    for half, pb, pres in ((0, pA, "bank_pA"), (1, pB, "bank_pB")):
                        def cbm(eng, e=e, half=half, pb=pb, tok0=tok0):
                            return eng.matmul(pb[:, :], onehot_sb[:, e * 128:(e + 1) * 128],
                                              combine_fm[:, tok0 + half * 512:tok0 + (half + 1) * 512],
                                              start=True, stop=True)
                        sch.add("pe", cbm, reads=("onehot", "combine_fm"), writes=(pres,), tag=f"f{gi}e{e}cb{half}")

                    def cbe(eng):
                        eng.tensor_copy(cb_bf[:, 0:512], pA[:, :])
                        return eng.tensor_copy(cb_bf[:, 512:1024], pB[:, :])
                    sch.add("dve", cbe, reads=("bank_pA", "bank_pB"),
                            writes=("cb_bf", "bank_pA", "bank_pB"), tag=f"f{gi}e{e}cbe")

                nich = 4
                for ich in range(nich):
                    HQ = [[(pA, "bank_pA"), (pB, "bank_pB"), (pS0v, "bank_pS0"), (pS1v, "bank_pS1")],
                          [(pO, "bank_pO"), (pT_f2, "bank_pT"), (pQT_f, "bank_pQT"), (pKT_f, "bank_pKT")]]
                    quad = HQ[ich % 2]
                    for tp in range(2):
                        wsrc = wr1 if tp == 0 else wr3
                        wres = f"wr1_{s1}" if tp == 0 else f"wr3_{s1}"
                        for ch in range(8):
                            for th in range(2):
                                pb, pres = quad[tp * 2 + th]
                                def hmm(eng, ich=ich, ch=ch, th=th, pb=pb, wsrc=wsrc, s1=s1, tok0=tok0):
                                    return eng.matmul(pb[:, :],
                                                      wsrc[:, s1, ch, ich * 128:(ich + 1) * 128],
                                                      Xln2_fm[:, ch, tok0 + th * 512:tok0 + (th + 1) * 512],
                                                      start=(ch == 0), stop=(ch == 7))
                                sch.add("pe", hmm, reads=(wres, "Xln2_all"), writes=(pres,),
                                        tag=f"f{gi}e{e}h{tp}_{ich}_{th}_{ch}")

                    if is_sh:
                        bcol1 = bs1T[:, (e - E) * 4 + ich:(e - E) * 4 + ich + 1]
                        bcol3 = bs3T[:, (e - E) * 4 + ich:(e - E) * 4 + ich + 1]
                        bres1, bres3 = "bs1T", "bs3T"
                    else:
                        bcol1 = biasT[:, ich, e:e + 1]
                        bcol3 = bias3T[:, ich, e:e + 1]
                        bres1, bres3 = "be1T", "be3T"

                    def sil(eng, bcol1=bcol1, quad=quad):
                        eng.activation(h1s[:, 0:512], quad[0][0][:, :], AF.Silu, bias=bcol1)
                        return eng.activation(h1s[:, 512:1024], quad[1][0][:, :], AF.Silu, bias=bcol1)
                    sch.add("act", sil, reads=(quad[0][1], quad[1][1], bres1),
                            writes=("h1s", quad[0][1], quad[1][1]), tag=f"f{gi}e{e}sil{ich}")

                    if not is_sh:
                        def hcf(eng):
                            return eng.tensor_tensor(hcm[:, :], h1s[:, :], cb_bf[:, :], ALU.mult)
                        sch.add("dve", hcf, reads=("h1s", "cb_bf"), writes=("hcm",), tag=f"f{gi}e{e}hc{ich}")
                        hsrc, hres = hcm, "hcm"
                    else:
                        hsrc, hres = h1s, "h1s"

                    def hfin(eng, ich=ich, ei=ei, bcol3=bcol3, hsrc=hsrc, quad=quad):
                        eng.scalar_tensor_tensor(h4[:, ei, ich, 0:512], quad[2][0][:, :],
                                                 bcol3, hsrc[:, 0:512], ALU.add, ALU.mult)
                        return eng.scalar_tensor_tensor(h4[:, ei, ich, 512:1024], quad[3][0][:, :],
                                                        bcol3, hsrc[:, 512:1024], ALU.add, ALU.mult)
                    sch.add("dve", hfin, reads=(quad[2][1], quad[3][1], hres, bres3),
                            writes=(f"h4_{ei}", quad[2][1], quad[3][1]), tag=f"f{gi}e{e}hf{ich}")

            for tl in range(8):
                par = tl % 2
                t_abs = gi * 8 + tl
                for ei, e in enumerate(rnd):
                    s2_ = w2slots[ei]
                    for ich in range(4):
                        for dhh in range(2):
                            pb, pres = YB[par][dhh]

                            def ymm(eng, ei=ei, tl=tl, ich=ich, dhh=dhh, s2_=s2_, pb=pb,
                                    start=(ei == 0 and ich == 0),
                                    stop=(ri < 4 and ei == len(rnd) - 1 and ich == 3)):
                                return eng.matmul(pb[:, :],
                                                  h4[:, ei, ich, tl * 128:(tl + 1) * 128],
                                                  wr2[:, s2_, ich, dhh * 512:(dhh + 1) * 512],
                                                  start=start, stop=stop)
                            sch.add("pe", ymm, reads=(f"h4_{ei}", f"wr2_{s2_}"), writes=(pres,),
                                    tag=f"f{gi}r{ri}yo{tl}_{ei}_{ich}_{dhh}")

                if ri == 4:
                    def bem(eng, tl=tl, t_abs=t_abs, par=par):
                        eng.matmul(YB[par][0][0][:, :],
                                   combine_fm[:, t_abs * 128:(t_abs + 1) * 128], be2_sb[:, 0:512],
                                   start=False, stop=False)
                        eng.matmul(YB[par][1][0][:, :], combine_fm[:, t_abs * 128:(t_abs + 1) * 128],
                                   be2_sb[:, 512:1024], start=False, stop=False)
                        eng.matmul(YB[par][0][0][:, :], ones_bf[0:1, :], bs2_sb[0:1, 0:512],
                                   start=False, stop=True)
                        return eng.matmul(YB[par][1][0][:, :], ones_bf[0:1, :], bs2_sb[0:1, 512:1024],
                                          start=False, stop=True)
                    sch.add("pe", bem, reads=("combine_fm", "be2", "bs2", "ones"),
                            writes=(YB[par][0][1], YB[par][1][1]), tag=f"f{gi}be2_{tl}")

                def yev(eng, tl=tl, par=par, ri=ri):
                    if ri == 0:
                        eng.tensor_copy(y_sb[:, tl, 0:512], YB[par][0][0][:, :])
                        return eng.tensor_copy(y_sb[:, tl, 512:1024], YB[par][1][0][:, :])
                    eng.tensor_tensor(y_sb[:, tl, 0:512], y_sb[:, tl, 0:512], YB[par][0][0][:, :], ALU.add)
                    return eng.tensor_tensor(y_sb[:, tl, 512:1024], y_sb[:, tl, 512:1024], YB[par][1][0][:, :], ALU.add)
                sch.add("dve", yev, reads=(YB[par][0][1], YB[par][1][1], f"ysb_{tl}"),
                        writes=(f"ysb_{tl}", YB[par][0][1], YB[par][1][1]), tag=f"f{gi}r{ri}yev{tl}")

                if ri == 4:
                    rstd4, nmr4, rres4 = ln_stats(y_sb[:, tl, :], f"ysb_{tl}", 6 + par, f"f{gi}t{tl}l4")

                    def f_ap4(eng, tl=tl, par=par, rstd4=rstd4, nmr4=nmr4):
                        eng.activation(YB[par][0][0][:, :], y_sb[:, tl, 0:512], AF.Identity,
                                       bias=nmr4, scale=rstd4)
                        return eng.activation(YB[par][1][0][:, :], y_sb[:, tl, 512:1024], AF.Identity,
                                              bias=nmr4, scale=rstd4)
                    sch.add("act", f_ap4, reads=(f"ysb_{tl}", rres4, YB[par][0][1], YB[par][1][1]),
                            writes=("lnf", YB[par][0][1], YB[par][1][1]), tag=f"f{gi}t{tl}ap4")

                    dma("sp", xres[:, 0, :], xscr.rearrange("(t p) d -> p t d", p=128)[:, t_abs, :],
                        ("xres0", f"xscr_{t_abs}"), ("xres0",), f"f{gi}xld{tl}")

                    def f_add(eng, par=par):
                        eng.tensor_tensor(xres[:, 0, 0:512], xres[:, 0, 0:512], YB[par][0][0][:, :], ALU.add)
                        return eng.tensor_tensor(xres[:, 0, 512:1024], xres[:, 0, 512:1024], YB[par][1][0][:, :], ALU.add)
                    sch.add("dve", f_add, reads=("lnf", "xres0", YB[par][0][1], YB[par][1][1]),
                            writes=("xres0", YB[par][0][1], YB[par][1][1]), tag=f"f{gi}out{tl}")

                    dma("sp", out_t.rearrange("(t p) d -> p t d", p=128)[:, t_abs, :],
                        xres[:, 0, :], ("xres0",), ("xres0", "out_done"), f"f{gi}st{tl}")

    sch.add("dve", lambda eng: eng.engine_nop(), reads=("out_done",), tag="final")

    globals()["DEBUG_TENSORS"] = dict(
        R_all=R_all, R_bf=R_bf, xtile=xtile)
    sch.finalize()
    sems = {}
    with ExitStack() as ctx:
        Ectx = ctx.enter_context
        for e in Sched.ENG:
            sems[e] = Ectx(nc.semaphore(f"sem_{e}"))
        for e in ("sp", "pool"):
            for q in range(Sched.NLANES):
                sems[f"{e}.q{q}"] = Ectx(nc.semaphore(f"sem_{e}_q{q}"))
        block = Ectx(nc.Block())

        @block.tensor
        def _(eng):
            sch.run_engine("pe", eng, sems)

        @block.vector
        def _(eng):
            sch.run_engine("dve", eng, sems)

        @block.scalar
        def _(eng):
            sch.run_engine("act", eng, sems)

        @block.gpsimd
        def _(eng):
            sch.run_engine("pool", eng, sems)

        @block.sync
        def _(eng):
            sch.run_engine("sp", eng, sems)
            for e in Sched.ENG:
                if e != "sp" and sch.final_counts.get(e):
                    eng.wait_ge(sems[e], sch.final_counts[e])
            for qe in ("sp", "pool"):
                nq = sch.lane_use[qe]
                for q in range(Sched.NLANES):
                    uses = nq // Sched.NLANES + (1 if q < nq % Sched.NLANES else 0)
                    if uses:
                        eng.wait_ge(sems[f"{qe}.q{q}"], 16 * uses)

    return nc


def _tables(core):
    pos_chunk = core % 4
    base = pos_chunk * TCORE
    inv = 1.0 / ROPE_BASE ** (np.arange(0, DH, 2, dtype=np.float64) / DH)
    gam = 1.0 - np.exp2(-5.0 - np.arange(H, dtype=np.float64))

    def rope_tabs(positions):
        th = positions[:, :, None] * inv[None, None, :]
        cos = np.cos(th)
        cos = np.concatenate([cos, cos], axis=2)
        sin = np.sin(th)
        sin_signed = np.concatenate([-sin, sin], axis=2)
        return cos.astype(np.float32), sin_signed.astype(np.float32)

    p = np.arange(128)
    own_pos = base + np.arange(NT)[None, :] * 128 + p[:, None]
    cos_o, sin_o = rope_tabs(own_pos)
    pre_pos = np.arange(NPT)[None, :] * 128 + p[:, None] - (PRE - base)
    pre_pos = np.maximum(pre_pos, 0)
    cos_p, sin_p = rope_tabs(pre_pos)
    cos_p = cos_p.astype(ml_dtypes.bfloat16)
    sin_p = sin_p.astype(ml_dtypes.bfloat16)

    i = np.arange(128)
    j = np.arange(128)
    diff = i[None, :] - j[:, None]
    dec = np.where(diff >= 0, gam[:, None, None] ** np.maximum(diff, 0)[None], 0.0) / 8.0
    decayT = np.transpose(dec, (1, 0, 2)).astype(ml_dtypes.bfloat16)

    gpow = np.zeros((128, D), dtype=np.float64)
    for h in range(H):
        gpow[:, h * DH:(h + 1) * DH] = (gam[h] ** (127 - i))[:, None]
    gpow = gpow.astype(ml_dtypes.bfloat16)

    g128 = np.zeros((64, H * DH), dtype=np.float64)
    for h in range(H):
        g128[:, h * DH:(h + 1) * DH] = gam[h] ** 128
    g128 = g128.astype(np.float32)

    gb = np.zeros((64, H, 128), dtype=np.float64)
    for h in range(H):
        gb[:, h, :] = gam[h] ** (i[None, :] + 1) / 8.0
    gbcast = gb.astype(ml_dtypes.bfloat16)

    onehot = np.zeros((16, 16 * 128), dtype=ml_dtypes.bfloat16)
    for e in range(16):
        onehot[e, e * 128:(e + 1) * 128] = 1.0

    return dict(cos_own=cos_o, sin_own=sin_o, cos_pre=cos_p, sin_pre=sin_p,
                decayT=decayT, gpow_k=gpow, g128=g128, gbcast=gbcast, onehot=onehot,
                ident_in=np.eye(128, dtype=ml_dtypes.bfloat16))


_NC_CACHE = {}


def kernel(L0_attn_x, L0_attn_wq, L0_attn_wk, L0_attn_wv, L0_attn_wg, L0_attn_wo,
           L0_ffn_gate_w, L0_ffn_gate_b, L0_ffn_We1, L0_ffn_be1, L0_ffn_We2,
           L0_ffn_be2, L0_ffn_We3, L0_ffn_be3, L0_ffn_ws1, L0_ffn_bs1,
           L0_ffn_ws2, L0_ffn_bs2, L0_ffn_ws3, L0_ffn_bs3):
    X = np.ascontiguousarray(np.asarray(L0_attn_x, dtype=np.float32))
    shared = dict(
        wq=np.ascontiguousarray(L0_attn_wq), wk=np.ascontiguousarray(L0_attn_wk),
        wv=np.ascontiguousarray(L0_attn_wv), wg=np.ascontiguousarray(L0_attn_wg),
        wo=np.ascontiguousarray(L0_attn_wo),
        gate_w=np.ascontiguousarray(L0_ffn_gate_w),
        gate_b=np.ascontiguousarray(L0_ffn_gate_b).reshape(1, E),
        We1=np.ascontiguousarray(L0_ffn_We1), be1=np.ascontiguousarray(L0_ffn_be1),
        We2=np.ascontiguousarray(L0_ffn_We2), be2=np.ascontiguousarray(L0_ffn_be2),
        We3=np.ascontiguousarray(L0_ffn_We3), be3=np.ascontiguousarray(L0_ffn_be3),
        ws1=np.ascontiguousarray(L0_ffn_ws1), bs1=np.ascontiguousarray(L0_ffn_bs1).reshape(1, D),
        ws2=np.ascontiguousarray(L0_ffn_ws2), bs2=np.ascontiguousarray(L0_ffn_bs2).reshape(1, D),
        ws3=np.ascontiguousarray(L0_ffn_ws3), bs3=np.ascontiguousarray(L0_ffn_bs3).reshape(1, D),
    )
    in_maps = []
    for c in range(NCORE):
        b, pos = c // 4, c % 4
        base = pos * TCORE
        xp = np.zeros((PRE, D), dtype=np.float32)
        if base > 0:
            xp[PRE - base:] = X[b, 0:base]
        m = dict(shared)
        m["x_own"] = np.ascontiguousarray(X[b, base:base + TCORE])
        m["x_pre"] = xp
        m.update(_tables(c))
        in_maps.append(m)

    if "nc" not in _NC_CACHE:
        _NC_CACHE["nc"] = build_nc()
    nc = _NC_CACHE["nc"]
    res = run_bass_kernel_spmd(nc, in_maps, core_ids=list(range(NCORE)))
    out = np.empty((B, S, D), dtype=np.float32)
    for c in range(NCORE):
        b, pos = c // 4, c % 4
        out[b, pos * TCORE:(pos + 1) * TCORE] = res.results[c]["out"]
    return out


# version 10
# speedup vs baseline: 1.0371x; 1.0371x over previous
import os
from collections import defaultdict
from contextlib import ExitStack

import numpy as np
import ml_dtypes

import concourse.bass as bass
import concourse.mybir as mybir
from concourse.bass_utils import run_bass_kernel_spmd

F32 = mybir.dt.float32
BF16 = mybir.dt.bfloat16
AF = mybir.ActivationFunctionType
ALU = mybir.AluOpType
AX = mybir.AxisListType

NCORE = 8
B, S, D = 2, 8192, 1024
H, DH = 16, 64
TCORE = S * B // NCORE
NT = TCORE // 128
PRE = 3 * TCORE
NPT = PRE // 128
E, INTER = 16, 512
EPS = 1e-5
ROPE_BASE = 10000.0

STAGE = os.environ.get("KSTAGE", "full")
KRET = os.environ.get("KRET", "d")
KOWNT = int(os.environ.get("KOWNT", str(NT)))

SBUF_BASE = 16512
SBUF_END = 229376


class Sched:
    ENG = ("pe", "dve", "act", "pool", "sp")
    NLANES = 8

    def __init__(self):
        self.ops = []
        self.last_writer = {}
        self.readers = defaultdict(list)
        self.lane_use = defaultdict(int)

    def add(self, engine, fn, reads=(), writes=(), dma=False, tag="", extra_deps=(),
            slow=False):
        idx = len(self.ops)
        deps = set(extra_deps)
        for r in reads:
            w = self.last_writer.get(r)
            if w is not None:
                deps.add(w)
        for r in writes:
            w = self.last_writer.get(r)
            if w is not None:
                deps.add(w)
            for rd in self.readers.get(r, ()):
                deps.add(rd)
        op = dict(idx=idx, eng=engine, fn=fn, deps=deps, dma=dma,
                  tag=tag, need_inc=False, inc_after=None, sem=engine,
                  lane_wait=None, slow=slow, vc={}, extra_vc=None)
        if dma:
            nq = self.lane_use[engine]
            lane = nq % self.NLANES
            use = nq // self.NLANES + 1
            self.lane_use[engine] = nq + 1
            op["sem"] = f"{engine}.q{lane}"
            op["need_inc"] = True
            op["inc_after"] = 16 * use
            if use > 1:
                op["lane_wait"] = (op["sem"], 16 * (use - 1))
        self.ops.append(op)
        for r in reads:
            self.readers[r].append(idx)
        for r in writes:
            self.last_writer[r] = idx
            self.readers[r] = [idx]
        return idx

    def barrier(self):
        last = {}
        for op in self.ops:
            last[op["eng"]] = op["idx"]
        lane_vc = {}
        for qe, nq in self.lane_use.items():
            for q in range(self.NLANES):
                uses = nq // self.NLANES + (1 if q < nq % self.NLANES else 0)
                if uses:
                    lane_vc[f"{qe}.q{q}"] = 16 * uses
        for e in self.ENG:
            deps = tuple(v for k, v in last.items() if k != e)
            i = self.add(e, lambda eng: None, extra_deps=deps, tag=f"bar_{e}")
            self.ops[i]["extra_vc"] = dict(lane_vc)

    def finalize(self):
        for op in self.ops:
            for d in op["deps"]:
                if d != op["idx"]:
                    self.ops[d]["need_inc"] = True
        cnt = defaultdict(int)
        for op in self.ops:
            if op["dma"]:
                continue
            if op["need_inc"]:
                cnt[op["eng"]] += 1
                op["inc_after"] = cnt[op["eng"]]
        self.final_counts = dict(cnt)
        for op in self.ops:
            vc = {}
            for d in sorted(op["deps"]):
                if d == op["idx"]:
                    continue
                dop = self.ops[d]
                for sname, v in dop["vc"].items():
                    if vc.get(sname, 0) < v:
                        vc[sname] = v
                if dop["inc_after"] is not None:
                    if vc.get(dop["sem"], 0) < dop["inc_after"]:
                        vc[dop["sem"]] = dop["inc_after"]
            if op["extra_vc"]:
                for sname, v in op["extra_vc"].items():
                    if vc.get(sname, 0) < v:
                        vc[sname] = v
            op["vc"] = vc

    def run_engine(self, engine_name, eng, sems):
        seen = defaultdict(int)
        for op in self.ops:
            if op["eng"] != engine_name:
                continue
            for sname, v in sorted(op["vc"].items()):
                if sname == engine_name:
                    continue
                if v > seen[sname]:
                    eng.wait_ge(sems[sname], v)
                    seen[sname] = v
            if engine_name == "dve":
                v = op["vc"].get("dve", 0)
                if v > seen["dve"]:
                    eng.wait_ge(sems["dve"], v)
                    seen["dve"] = v
            if op["lane_wait"] is not None:
                sname, v = op["lane_wait"]
                if v > seen[sname]:
                    eng.wait_ge(sems[sname], v)
                    seen[sname] = v
            inst = op["fn"](eng)
            if op["need_inc"]:
                assert inst is not None, op["tag"]
                inst.then_inc(sems[op["sem"]], 16 if op["dma"] else 1)


class Arena:
    def __init__(self, nc, base=SBUF_BASE):
        self.nc = nc
        self.off = base
        self.hi = base

    def alloc(self, name, shape, dtype, align=64):
        per_part = int(np.prod(shape[1:])) * mybir.dt.size(dtype)
        self.off = (self.off + align - 1) // align * align
        t = self.nc.alloc_sbuf_tensor_at(name, list(shape), dtype, offset=self.off)
        self.off += per_part
        self.hi = max(self.hi, self.off)
        assert self.off <= SBUF_END, f"SBUF overflow at {name}: {self.off}"
        return t


def build_nc():
    nc = bass.Bass("TRN2", target_bir_lowering=False)

    def din(name, shape, dt=F32):
        return nc.dram_tensor(name, list(shape), dt, kind="ExternalInput").ap()

    x_own = din("x_own", [TCORE, D])
    x_pre = din("x_pre", [PRE, D])
    w_in = {n: din(n, [D, D]) for n in ("wq", "wk", "wv", "wg", "wo")}
    gate_w = din("gate_w", [E, D])
    gate_b = din("gate_b", [1, E])
    We1, We3 = din("We1", [E, D, INTER]), din("We3", [E, D, INTER])
    We2 = din("We2", [E, INTER, D])
    be1, be3 = din("be1", [E, INTER]), din("be3", [E, INTER])
    be2 = din("be2", [E, D])
    ws1, ws2, ws3 = din("ws1", [D, D]), din("ws2", [D, D]), din("ws3", [D, D])
    bs1, bs2, bs3 = din("bs1", [1, D]), din("bs2", [1, D]), din("bs3", [1, D])
    ident_in = din("ident_in", [128, 128], BF16)
    cos_own = din("cos_own", [128, NT, DH])
    sin_own = din("sin_own", [128, NT, DH])
    cos_pre_h = din("cos_pre", [128, NPT, DH], BF16)
    sin_pre_h = din("sin_pre", [128, NPT, DH], BF16)
    decayT_h = din("decayT", [128, H, 128], BF16)
    gpow_h = din("gpow_k", [128, D], BF16)
    g128_h = din("g128", [64, H * DH])
    gbcast_h = din("gbcast", [64, H, 128], BF16)
    onehot_h = din("onehot", [16, 16 * 128], BF16)

    out_t = nc.dram_tensor("out", [TCORE, D], F32, kind="ExternalOutput").ap()
    xscr = nc.dram_tensor("xscr", [TCORE, D], F32, kind="Internal").ap()
    wbf = {n: nc.dram_tensor(f"{n}_bf", [D, D], BF16, kind="Internal").ap()
           for n in ("wq", "wk", "wv", "wg", "wo")}
    gwbf = nc.dram_tensor("gw_bf", [E, D], BF16, kind="Internal").ap()

    A = Arena(nc)
    sch = Sched()

    ident = A.alloc("ident", [128, 128], BF16)
    ident_f = A.alloc("ident_f", [128, 128], F32)
    R_all = A.alloc("R_all", [64, H, DH], F32)
    R_bf = A.alloc("R_bf", [64, H, DH], BF16)
    g128_sb = A.alloc("g128_sb", [64, H * DH], F32)
    gbcast_sb = A.alloc("gbcast_sb", [64, H, 128], BF16)
    gpow_sb = A.alloc("gpow_sb", [128, D], BF16)
    decay_sb = A.alloc("decay_sb", [128, H, 128], BF16)
    combine_fm = A.alloc("combine_fm", [16, TCORE], BF16)
    onehot_sb = A.alloc("onehot_sb", [16, 16 * 128], BF16)
    p_all = A.alloc("p_all", [128, NT, E], F32)
    ones_bf = A.alloc("ones_bf", [1, 128], BF16)
    ones_f = A.alloc("ones_f", [1, 128], F32)
    gateb_sb = A.alloc("gateb_sb", [1, E], F32)
    gb_bcast = A.alloc("gb_bcast", [128, E], F32)
    stat = A.alloc("stat", [128, 64], F32)
    bncols = A.alloc("bncols", [128, 8, 12], F32)
    mnt = A.alloc("mnt", [128, 8, 4], F32)
    biasT = A.alloc("biasT", [128, 4, E], F32)
    bias3T = A.alloc("bias3T", [128, 4, E], F32)
    bs1T = A.alloc("bs1T", [128, 8], F32)
    bs3T = A.alloc("bs3T", [128, 8], F32)
    be2_sb = A.alloc("be2_sb", [16, D], BF16)
    bs2_sb = A.alloc("bs2_sb", [1, D], BF16)
    Xln2_fm = A.alloc("Xln2_fm", [128, 8, TCORE], BF16)
    gsc = A.alloc("gsc", [128, 48], F32)
    s2t = A.alloc("s2t", [128, E], F32)
    comb_t = A.alloc("comb_t", [128, E], BF16)
    epscol = A.alloc("epscol", [128, 1], F32)
    persist_hi = A.off

    attn_base = A.off
    wT = {n: A.alloc(f"{n}T", [128, 8, D], BF16) for n in ("wq", "wk", "wv", "wg", "wo")}
    gateT = A.alloc("gateT", [128, 8, E], BF16)
    cos_own_sb = A.alloc("cos_own_sb", [128, NT, DH], F32)
    sin_own_sb = A.alloc("sin_own_sb", [128, NT, DH], F32)
    cos_pre_sb = A.alloc("cos_pre_sb", [128, 16, DH], BF16)
    sin_pre_sb = A.alloc("sin_pre_sb", [128, 16, DH], BF16)
    xtile = A.alloc("xtile", [128, 2, D], F32)
    ropeA = A.alloc("ropeA", [128, D], F32)
    ropeB = A.alloc("ropeB", [128, D], F32)
    xln_bf = A.alloc("xln_bf", [128, 2, D], BF16)
    xlnfm = A.alloc("xlnfm", [128, 1, D], BF16)
    q_tm = A.alloc("q_tm", [128, D], BF16)
    k_tm = A.alloc("k_tm", [128, D], BF16)
    v_tm = A.alloc("v_tm", [128, D], BF16)
    ks_bf = A.alloc("ks_bf", [128, D], BF16)
    qfm = A.alloc("qfm", [64, H, 128], BF16)
    kfm = A.alloc("kfm", [64, H, 128], BF16)
    qs_bf = A.alloc("qs_bf", [64, H, 128], BF16)
    g_tm = A.alloc("g_tm", [128, D], BF16)
    m_tm = A.alloc("m_tm", [128, D], BF16)
    mfm = q_tm
    masked = nc.alloc_sbuf_tensor_at("masked", [128, 2, 512], BF16, offset=nc.lookup_mloc(ropeA).addr)
    ln_scr = ropeB
    attn_hi = A.off

    A.off = attn_base
    y_sb = A.alloc("y_sb", [128, 8, D], F32)
    h4 = A.alloc("h4", [128, 4, 4, 1024], BF16)
    wr1 = A.alloc("wr1", [128, 2, 8, INTER], BF16)
    wr3 = A.alloc("wr3", [128, 2, 8, INTER], BF16)
    wr2 = A.alloc("wr2", [128, 4, 4, D], BF16)
    h1s = A.alloc("h1s", [128, 1024], BF16)
    biasrow = nc.alloc_sbuf_tensor_at("biasrow", [16, INTER], F32, offset=nc.lookup_mloc(h1s).addr)
    hcm = A.alloc("hcm", [128, 1024], BF16)
    cb_bf = A.alloc("cb_bf", [128, 1024], BF16)
    xres = A.alloc("xres", [128, 1, D], F32)
    ffn_hi = A.off
    assert max(attn_hi, ffn_hi) <= SBUF_END, (attn_hi, ffn_hi)

    pT = nc.alloc_psum_tensor("pT", [128, 8, 128], BF16)
    pA = nc.alloc_psum_tensor("pA", [128, 512], F32)
    pB = nc.alloc_psum_tensor("pB", [128, 512], F32)
    pQT = nc.alloc_psum_tensor("pQT", [128, 8, 128], BF16)
    pKT = nc.alloc_psum_tensor("pKT", [128, 8, 128], BF16)
    pS0 = nc.alloc_psum_tensor("pS0", [128, 512], F32)
    pS1 = nc.alloc_psum_tensor("pS1", [128, 512], F32)
    pO = nc.alloc_psum_tensor("pO", [128, 512], F32)
    pT_f = pT.ap().rearrange("p c n -> p (c n)").bitcast(F32)
    pT_f2 = pT_f
    pQT_f = pQT.ap().rearrange("p c n -> p (c n)").bitcast(F32)
    pKT_f = pKT.ap().rearrange("p c n -> p (c n)").bitcast(F32)
    YB = [[(pO, "bank_pO"), (pT_f, "bank_pT")],
          [(pQT_f, "bank_pQT"), (pKT_f, "bank_pKT")]]

    def dma(engine, out, in_, reads, writes, tag, **kw):
        def fn(eng, out=out, in_=in_, kw=kw):
            return eng.dma_start(out=out, in_=in_, **kw)
        return sch.add(engine, fn, reads=reads, writes=writes, dma=True, tag=tag)

    def ln_stats(src_ap, src_res, slot, tag):
        mean = stat[:, slot * 8 + 0:slot * 8 + 1]
        var = stat[:, slot * 8 + 1:slot * 8 + 2]
        stdv = stat[:, slot * 8 + 2:slot * 8 + 3]
        rstd = stat[:, slot * 8 + 3:slot * 8 + 4]
        nmr = stat[:, slot * 8 + 4:slot * 8 + 5]
        tmp = stat[:, slot * 8 + 5:slot * 8 + 6]
        bnc = bncols[:, slot, :]
        sres = f"stat{slot}"
        bres = f"bnc{slot}"

        def f_bn(eng):
            eng.bn_stats(bnc[:, 0:6], src_ap[:, 0:512])
            return eng.bn_stats(bnc[:, 6:12], src_ap[:, 512:1024])
        sch.add("dve", f_bn, reads=(src_res,), writes=(bres,), tag=f"{tag}_bn", slow=True)

        msum = stat[:, slot * 8 + 6:slot * 8 + 7]
        v1c = stat[:, slot * 8 + 7:slot * 8 + 8]
        mntmp = mnt[:, slot, :]
        def f_c1(eng):
            eng.tensor_reduce(msum, bnc[:, 1::3], AX.X, ALU.add)
            eng.tensor_reduce(v1c, bnc[:, 2::3], AX.X, ALU.add)
            return eng.tensor_tensor(mntmp, bnc[:, 1::3], bnc[:, 1::3], ALU.mult)
        sch.add("dve", f_c1, reads=(bres,), writes=(sres + "a", bres + "t"), tag=f"{tag}_c1")

        def f_c2(eng):
            return eng.tensor_reduce(tmp, mntmp, AX.X, ALU.add)
        sch.add("dve", f_c2, reads=(bres + "t",), writes=(sres + "b",), tag=f"{tag}_c2")

        def f_c3(eng):
            eng.tensor_scalar_mul(mean, msum, 0.25)
            return eng.scalar_tensor_tensor(v1c, tmp, 256.0, v1c, ALU.mult, ALU.add)
        sch.add("dve", f_c3, reads=(sres + "a", sres + "b"), writes=(sres + "c",), tag=f"{tag}_c3")

        def f_c4(eng):
            return eng.tensor_tensor(tmp, mean, mean, ALU.mult)
        sch.add("dve", f_c4, reads=(sres + "c",), writes=(sres + "d",), tag=f"{tag}_c4")

        def f_c5(eng):
            return eng.scalar_tensor_tensor(var, v1c, 1.0 / D, tmp, ALU.mult, ALU.subtract)
        sch.add("dve", f_c5, reads=(sres + "c", sres + "d"), writes=(sres,), tag=f"{tag}_c5")

        def f_sqrt(eng):
            return eng.activation(stdv, var, AF.Sqrt, bias=epscol[:, :])
        sch.add("act", f_sqrt, reads=(sres, "eps"), writes=(sres + "s",), tag=f"{tag}_sq")

        def f_recip(eng):
            return eng.reciprocal(rstd, stdv)
        sch.add("dve", f_recip, reads=(sres + "s",), writes=(sres + "q",), tag=f"{tag}_rc", slow=True)

        def f_nmr(eng):
            return eng.scalar_tensor_tensor(nmr, mean, -1.0, rstd, ALU.mult, ALU.mult)
        sch.add("dve", f_nmr, reads=(sres + "q", sres), writes=(sres + "r",), tag=f"{tag}_r")
        return rstd, nmr, sres + "r"

    def ln_apply(dst_ap, dst_res, src_ap, src_res, rstd, nmr, rres, tag):
        def f(eng):
            return eng.activation(dst_ap, src_ap, AF.Identity, bias=nmr, scale=rstd)
        sch.add("act", f, reads=(src_res, rres), writes=(dst_res,), tag=f"{tag}_ap")

    def transpose8(src_sb, src_res, dst_ap, dst_res, tag, pbank=pT, pres="bank_pT"):
        for ch in range(8):
            def tr(eng, ch=ch):
                return eng.transpose(pbank[:, ch, :], src_sb[:, ch * 128:(ch + 1) * 128], ident[:, :])
            sch.add("pe", tr, reads=(src_res, "ident"), writes=(pres,), tag=f"{tag}_t{ch}")

        def ev(eng):
            d3 = dst_ap if dst_ap.ndim == 3 else dst_ap.rearrange("p (c n) -> p c n", n=128)
            return eng.tensor_copy(d3, pbank[:, :, :])
        sch.add("dve", ev, reads=(pres,), writes=(dst_res, pres), tag=f"{tag}_ev")

    PAIRS = [((pA, "bank_pA"), (pB, "bank_pB")), ((pS0, "bank_pS0"), (pS1, "bank_pS1"))]

    def proj(wname, xf_ap, xf_res, tag, pair=0):
        (b0, r0), (b1, r1) = PAIRS[pair]
        for ch in range(8):
            for half, pb, pres in ((0, b0, r0), (1, b1, r1)):
                def mm(eng, ch=ch, half=half, pb=pb, wname=wname, xf_ap=xf_ap):
                    return eng.matmul(pb[:, :], xf_ap[:, ch * 128:(ch + 1) * 128],
                                      wT[wname][:, ch, half * 512:(half + 1) * 512],
                                      start=(ch == 0), stop=(ch == 7))
                sch.add("pe", mm, reads=(f"{wname}T", xf_res), writes=(pres,), tag=f"{tag}{half}_{ch}")

    def rope(dst_ap, dst_res, cos_sb, sin_sb, cres, ti, tag, pair=0):
        (pa_t, r0), (pb_t, r1) = PAIRS[pair]
        c8 = cos_sb[:, ti, :][:, None, :].broadcast_to([128, 8, DH])
        s0 = sin_sb[:, ti, 0:32][:, None, :].broadcast_to([128, 8, 32])
        s1 = sin_sb[:, ti, 32:64][:, None, :].broadcast_to([128, 8, 32])

        def f1(eng):
            eng.tensor_tensor(ropeA.ap().rearrange("p (h d) -> p h d", d=DH)[:, 0:8, :],
                              pa_t.ap().rearrange("p (h d) -> p h d", d=DH), c8, ALU.mult)
            return eng.tensor_tensor(ropeA.ap().rearrange("p (h d) -> p h d", d=DH)[:, 8:16, :],
                                     pb_t.ap().rearrange("p (h d) -> p h d", d=DH), c8, ALU.mult)
        sch.add("dve", f1, reads=(r0, r1, cres), writes=("ropeA",), tag=f"{tag}_c")

        def f2(eng):
            rv = ropeB.ap().rearrange("p (h s d) -> p h s d", s=2, d=32)
            pa = pa_t.ap().rearrange("p (h s d) -> p h s d", s=2, d=32)
            pb = pb_t.ap().rearrange("p (h s d) -> p h s d", s=2, d=32)
            eng.tensor_tensor(rv[:, 0:8, 0, :], pa[:, :, 1, :], s0, ALU.mult)
            eng.tensor_tensor(rv[:, 0:8, 1, :], pa[:, :, 0, :], s1, ALU.mult)
            eng.tensor_tensor(rv[:, 8:16, 0, :], pb[:, :, 1, :], s0, ALU.mult)
            return eng.tensor_tensor(rv[:, 8:16, 1, :], pb[:, :, 0, :], s1, ALU.mult)
        sch.add("dve", f2, reads=(r0, r1, cres), writes=("ropeB",), tag=f"{tag}_s")

        def f3(eng):
            return eng.tensor_add(dst_ap, ropeA[:, :], ropeB[:, :])
        sch.add("dve", f3, reads=("ropeA", "ropeB"), writes=(dst_res, r0, r1), tag=f"{tag}_a")

    def state_update(vsrc_ap, vsrc_res, ksrc_ap, ksrc_res, tag):
        def ksf(eng):
            return eng.tensor_tensor(ks_bf[:, :], ksrc_ap, gpow_sb[:, :], ALU.mult)
        sch.add("dve", ksf, reads=(ksrc_res, "gpow"), writes=("ks_bf",), tag=f"{tag}_ks")

        def rup1(eng):
            return eng.tensor_tensor(R_all.ap().rearrange("p h d -> p (h d)"),
                                     R_all.ap().rearrange("p h d -> p (h d)"),
                                     g128_sb[:, :], ALU.mult)
        sch.add("dve", rup1, reads=("g128", "R_all"), writes=("R_all",), tag=f"{tag}_rup1")

        for rnd in range(2):
            for hh in range(8):
                h = rnd * 8 + hh

                def smm(eng, hh=hh, h=h):
                    return eng.matmul(pT_f[0:64, hh * 64:(hh + 1) * 64],
                                      ks_bf[:, h * DH:(h + 1) * DH], vsrc_ap[:, h * DH:(h + 1) * DH],
                                      start=True, stop=True)
                sch.add("pe", smm, reads=("ks_bf", vsrc_res), writes=("bank_pT",), tag=f"{tag}_smm{h}")

            def rup2(eng, rnd=rnd):
                return eng.tensor_tensor(
                    R_all[:, rnd * 8:(rnd + 1) * 8, :].rearrange("p h d -> p (h d)"),
                    R_all[:, rnd * 8:(rnd + 1) * 8, :].rearrange("p h d -> p (h d)"),
                    pT_f[0:64, :], ALU.add)
            sch.add("dve", rup2, reads=("bank_pT", "R_all"), writes=("R_all", "bank_pT"), tag=f"{tag}_rup2{rnd}")

        def rbf(eng):
            return eng.tensor_copy(R_bf[:, :, :], R_all[:, :, :])
        sch.add("dve", rbf, reads=("R_all",), writes=("R_bf",), tag=f"{tag}_rbf")

    dma("sp", ident[:, :], ident_in[:, :], (), ("ident",), "ld_ident")
    dma("sp", cos_own_sb[:, :, :], cos_own[:, :, :], (), ("cos_own",), "ld_cos")
    dma("sp", sin_own_sb[:, :, :], sin_own[:, :, :], (), ("cos_own",), "ld_sin")
    dma("sp", decay_sb[:, :, :], decayT_h[:, :, :], (), ("decay",), "ld_decay")
    dma("sp", gpow_sb[:, :], gpow_h[:, :], (), ("gpow",), "ld_gpow")
    dma("sp", g128_sb[:, :], g128_h[:, :], (), ("g128",), "ld_g128")
    dma("sp", gbcast_sb[:, :, :], gbcast_h[:, :, :], (), ("gbcast",), "ld_gb")
    dma("sp", gateb_sb[:, :], gate_b[:, :], (), ("gateb",), "ld_gateb")
    dma("sp", onehot_sb[:, :], onehot_h[:, :], (), ("onehot",), "ld_oh")
    sch.add("dve", lambda eng: eng.memset(epscol[:, :], EPS), writes=("eps",), tag="eps")
    sch.add("dve", lambda eng: eng.memset(ones_bf[:, :], 1.0), writes=("ones",), tag="ones")
    sch.add("dve", lambda eng: eng.memset(ones_f[:, :], 1.0), writes=("ones_f",), tag="ones_f")
    sch.add("dve", lambda eng: eng.memset(R_all[:, :, :], 0.0), writes=("R_all",), tag="R0")
    sch.add("dve", lambda eng: eng.memset(R_bf[:, :, :], 0.0), writes=("R_bf",), tag="R0b")
    sch.add("dve", lambda eng: eng.tensor_copy(ident_f[:, :], ident[:, :]),
            reads=("ident",), writes=("identf",), tag="identf")

    for n in ("wq", "wk", "wv", "wg", "wo"):
        dma("pool", wbf[n][:, :], w_in[n][:, :], (), (f"{n}bf",), f"cast_{n}")
        for ch in range(8):
            dma("sp", wT[n][:, ch, :], wbf[n][0:D, ch * 128:(ch + 1) * 128],
                (f"{n}bf",), (f"{n}T",), f"tr_{n}{ch}", transpose=True)
    dma("pool", gwbf[:, :], gate_w[:, :], (), ("gwbf",), "cast_gw")
    for ch in range(8):
        dma("sp", gateT[:, ch, :], gwbf[0:E, ch * 128:(ch + 1) * 128],
            ("gwbf",), ("gateT",), f"tr_gw{ch}", transpose=True)
    dma("pool", be2_sb[:, :], be2[:, :], (), ("be2",), "cast_be2")
    dma("pool", bs2_sb[:, :], bs2[:, :], (), ("bs2",), "cast_bs2")

    def bias_transpose(name, hbm_view, rows, cols, dst_fn, res):
        dma("sp", biasrow[0:rows, 0:cols], hbm_view, ("brow",), ("brow",), f"ld_{name}")
        for j in range(cols // 128):
            def tr(eng, j=j, rows=rows):
                return eng.transpose(pA[0:128, 0:rows], biasrow[0:rows, j * 128:(j + 1) * 128], ident_f[0:rows, 0:rows])
            sch.add("pe", tr, reads=("brow", "identf"), writes=("bank_pA",), tag=f"trb_{name}{j}")

            def ev(eng, j=j, rows=rows):
                return eng.tensor_copy(dst_fn(j, rows), pA[0:128, 0:rows])
            sch.add("dve", ev, reads=("bank_pA",), writes=(res, "bank_pA"), tag=f"evb_{name}{j}")

    bias_transpose("be1", be1[:, :], E, INTER, lambda j, r: biasT[:, j, 0:r], "be1T")
    bias_transpose("be3", be3[:, :], E, INTER, lambda j, r: bias3T[:, j, 0:r], "be3T")
    bias_transpose("bs1", bs1.rearrange("o (c p) -> (o c) p", p=128), 8, 128,
                   lambda j, r: bs1T[:, 0:r], "bs1T")
    bias_transpose("bs3", bs3.rearrange("o (c p) -> (o c) p", p=128), 8, 128,
                   lambda j, r: bs3T[:, 0:r], "bs3T")

    def gbb(eng):
        return eng.matmul(pA[:, 0:E], ones_f[0:1, :], gateb_sb[0:1, :], start=True, stop=True)
    sch.add("pe", gbb, reads=("ones_f", "gateb"), writes=("bank_pA",), tag="gbb")
    sch.add("dve", lambda eng: eng.tensor_copy(gb_bcast[:, :], pA[:, 0:E]),
            reads=("bank_pA",), writes=("gbb_r", "bank_pA"), tag="gbb_ev")

    PRE_ON = STAGE not in ("ln1", "qproj", "own2")
    def pre_A(pt):
        buf = pt % 2
        xa = xtile[:, buf, :]
        xr = f"xtile{buf}"
        dma("sp", xa, x_pre.rearrange("(t p) d -> p t d", p=128)[:, pt, :],
            (xr,), (xr,), f"pre_ld{pt}")
        st = ln_stats(xa, xr, buf, f"pre{pt}")
        ln_apply(xln_bf[:, buf, :], f"xln{buf}", xa, xr, st[0], st[1], st[2], f"pre{pt}")
        return st

    def pre_B(pt, st):
        buf = pt % 2
        ti = pt % 16
        xa = xtile[:, buf, :]
        xr = f"xtile{buf}"
        rstd, nmr, rres = st
        transpose8(xln_bf[:, buf, :], f"xln{buf}", xlnfm[:, 0, :], "xlnfm0", f"pre{pt}")
        proj("wk", xlnfm[:, 0, :], "xlnfm0", f"pre{pt}k", pair=0)
        rope(k_tm[:, :], "k_tm", cos_pre_sb, sin_pre_sb, "cos_pre", ti, f"pre{pt}kr", pair=0)
        proj("wv", xlnfm[:, 0, :], "xlnfm0", f"pre{pt}v", pair=1)

        def vev(eng):
            eng.tensor_copy(v_tm[:, 0:512], pS0[:, :])
            return eng.tensor_copy(v_tm[:, 512:1024], pS1[:, :])
        sch.add("dve", vev, reads=("bank_pS0", "bank_pS1"),
                writes=("v_tm", "bank_pS0", "bank_pS1"), tag=f"pre{pt}_vev")
        state_update(v_tm[:, :], "v_tm", k_tm[:, :], "k_tm", f"pre{pt}")

    if PRE_ON:
        dma("sp", cos_pre_sb[:, :, :], cos_pre_h[:, 0:16, :], ("cos_pre",), ("cos_pre",), "ld_cosp0")
        dma("sp", sin_pre_sb[:, :, :], sin_pre_h[:, 0:16, :], ("cos_pre",), ("cos_pre",), "ld_sinp0")
        st = pre_A(0)
        for pt in range(NPT):
            stn = pre_A(pt + 1) if pt + 1 < NPT else None
            pre_B(pt, st)
            if pt + 1 < NPT and (pt + 1) % 16 == 0:
                ch = (pt + 1) // 16
                dma("sp", cos_pre_sb[:, :, :], cos_pre_h[:, ch * 16:(ch + 1) * 16, :],
                    ("cos_pre",), ("cos_pre",), f"ld_cosp{ch}")
                dma("sp", sin_pre_sb[:, :, :], sin_pre_h[:, ch * 16:(ch + 1) * 16, :],
                    ("cos_pre",), ("cos_pre",), f"ld_sinp{ch}")
            st = stn

    if STAGE == "pre":
        def rdump(eng):
            return eng.tensor_copy(xtile[0:64, 0, 0:1024], R_all.ap().rearrange("p h d -> p (h d)"))
        sch.add("dve", rdump, reads=("R_all",), writes=("xtile0",), tag="rdump")
    if STAGE == "pre2":
        def rdump2(eng):
            return eng.tensor_copy(xtile[0:64, 0, 0:1024], R_bf.ap().rearrange("p h d -> p (h d)"))
        sch.add("dve", rdump2, reads=("R_bf",), writes=("xtile0",), tag="rdump2")
        dma("sp", out_t.rearrange("(t p) d -> p t d", p=128)[:, 0, :],
            xtile[:, 0, :], ("xtile0",), ("out_done",), "dbgR2")
        dma("sp", out_t.rearrange("(t p) d -> p t d", p=128)[:, 0, :],
            xtile[:, 0, :], ("xtile0",), ("out_done",), "dbgR")

    def emit_xln2_T(t):
        transpose8(ks_bf[:, :], "ks_bf",
                   Xln2_fm[:, :, t * 128:(t + 1) * 128],
                   f"Xln2_{t}", f"o{t}x2T")

    def own_A(t):
        buf = t % 2
        xa = xtile[:, buf, :]
        xr = f"xtile{buf}"
        dma("sp", xa, x_own.rearrange("(t p) d -> p t d", p=128)[:, t, :],
            (xr,), (xr,), f"own_ld{t}")
        st = ln_stats(xa, xr, buf, f"o{t}")
        ln_apply(xln_bf[:, buf, :], f"xln{buf}", xa, xr, st[0], st[1], st[2], f"o{t}")
        return st

    def own_B(t, st):
        buf = t % 2
        xa = xtile[:, buf, :]
        xr = f"xtile{buf}"
        rstd, nmr, rres = st
        if STAGE == "ln1":
            dma("pool", out_t.rearrange("(t p) d -> p t d", p=128)[:, t, :],
                xln_bf[:, buf, :], (f"xln{buf}",), (f"xln{buf}", "out_done"), f"dbg{t}")
            return
        if STAGE == "ln1dbg":
            dma("sp", out_t.rearrange("(t p) d -> p t d", p=128)[:, t, 0:8],
                stat[:, buf * 8:buf * 8 + 8], (f"stat{buf}r",), ("out_done",), f"dbgs{t}")
            dma("sp", out_t.rearrange("(t p) d -> p t d", p=128)[:, t, 8:20],
                bncols[:, buf, :], (f"bnc{buf}",), ("out_done",), f"dbgb{t}")
            dma("sp", out_t.rearrange("(t p) d -> p t d", p=128)[:, t, 32:1024],
                xtile[:, buf, 32:1024], (xr,), ("out_done",), f"dbgx{t}")
            return
        transpose8(xln_bf[:, buf, :], f"xln{buf}", xlnfm[:, 0, :], "xlnfm0", f"o{t}")
        if t == 0 and STAGE == "ret" and KRET == "dumpR":
            def rsnap(eng):
                return eng.tensor_copy(xln_bf[0:64, 0, :], R_bf.ap().rearrange("p h d -> p (h d)"))
            sch.add("dve", rsnap, reads=("R_bf",), writes=("xln0",), tag="rsnap")
            dma("pool", out_t.rearrange("(t p) d -> p t d", p=128)[:, 15, :],
                xln_bf[:, 0, :], ("xln0",), ("xln0", "rsnapd"), "rsnapdma")
        xf, xfr = xlnfm[:, 0, :], "xlnfm0"

        proj("wq", xf, xfr, f"o{t}q")
        rope(q_tm[:, :], "q_tm", cos_own_sb, sin_own_sb, "cos_own", t, f"o{t}qr")
        if STAGE == "qproj":
            dma("pool", out_t.rearrange("(t p) d -> p t d", p=128)[:, t, :],
                q_tm[:, :], ("q_tm",), ("q_tm", "out_done"), f"dbg{t}")
            return
        proj("wk", xf, xfr, f"o{t}k")
        rope(k_tm[:, :], "k_tm", cos_own_sb, sin_own_sb, "cos_own", t, f"o{t}kr")
        proj("wv", xf, xfr, f"o{t}v")

        def vev(eng):
            eng.tensor_copy(v_tm[:, 0:512], pA[:, :])
            return eng.tensor_copy(v_tm[:, 512:1024], pB[:, :])
        sch.add("dve", vev, reads=("bank_pA", "bank_pB"),
                writes=("v_tm", "bank_pA", "bank_pB"), tag=f"o{t}_vev")

        proj("wg", xf, xfr, f"o{t}g")

        def gev(eng):
            eng.activation(g_tm[:, 0:512], pA[:, :], AF.Silu)
            return eng.activation(g_tm[:, 512:1024], pB[:, :], AF.Silu)
        sch.add("act", gev, reads=("bank_pA", "bank_pB"),
                writes=("g_tm", "bank_pA", "bank_pB"), tag=f"o{t}_gev")

        for rnd in range(2):
            for hh in range(8):
                h = rnd * 8 + hh

                def qtr(eng, hh=hh, h=h):
                    return eng.transpose(pQT[0:64, hh, :], q_tm[:, h * DH:(h + 1) * DH], ident[:, :])
                sch.add("pe", qtr, reads=("q_tm", "ident"), writes=("bank_pQT",), tag=f"o{t}_qT{h}")

                def ktr(eng, hh=hh, h=h):
                    return eng.transpose(pKT[0:64, hh, :], k_tm[:, h * DH:(h + 1) * DH], ident[:, :])
                sch.add("pe", ktr, reads=("k_tm", "ident"), writes=("bank_pKT",), tag=f"o{t}_kT{h}")

            def qev(eng, rnd=rnd):
                return eng.tensor_copy(qfm[:, rnd * 8:(rnd + 1) * 8, :], pQT[0:64, :, :])
            sch.add("dve", qev, reads=("bank_pQT",), writes=("qfm", "bank_pQT"), tag=f"o{t}_qev{rnd}")

            def kev(eng, rnd=rnd):
                return eng.tensor_copy(kfm[:, rnd * 8:(rnd + 1) * 8, :], pKT[0:64, :, :])
            sch.add("dve", kev, reads=("bank_pKT",), writes=("kfm", "bank_pKT"), tag=f"o{t}_kev{rnd}")

        if STAGE == "own2":
            dma("pool", out_t.rearrange("(t p) d -> p t d", p=128)[:, t, 0:512],
                qfm.ap().rearrange("p h n -> p (h n)")[:, 0:512], ("qfm",), ("qfm", "out_done"), f"dbg{t}")
            return

        def qsf(eng):
            return eng.tensor_tensor(qs_bf[:, :, :], qfm[:, :, :], gbcast_sb[:, :, :], ALU.mult)
        sch.add("dve", qsf, reads=("qfm", "gbcast"), writes=("qs_bf",), tag=f"o{t}_qs")

        for g4 in range(4 if KRET != "a0" else 0):
            ps, pres = (pS0, "bank_pS0") if g4 % 2 == 0 else (pS1, "bank_pS1")
            psv = ps.ap().rearrange("p (c n) -> p c n", n=128)
            for hh in range(4):
                h = g4 * 4 + hh
                cp, par = h // 2, h % 2

                def scmm(eng, h=h, hh=hh, psv=psv):
                    return eng.matmul(psv[:, hh, :],
                                      kfm[0:64, h, :], qfm[0:64, h, :],
                                      start=True, stop=True)
                sch.add("pe", scmm, reads=("kfm", "qfm"), writes=(pres,), tag=f"o{t}_sc{h}")

            def mskf(eng, g4=g4, ps=ps):
                return eng.tensor_tensor(masked[:, g4 % 2, :], ps[:, :],
                                         decay_sb[:, 4 * g4:4 * (g4 + 1), :].rearrange("p c n -> p (c n)"),
                                         ALU.mult)
            sch.add("dve", mskf, reads=(pres, "decay"), writes=("ropeA", pres), tag=f"o{t}_msk{g4}")

            for hh in range(4 if KRET not in ("a",) else 0):
                h = g4 * 4 + hh
                cp, par = h // 2, h % 2

                def omm1(eng, g4=g4, hh=hh, h=h):
                    return eng.matmul(pO[:, (h % 8) * 64:(h % 8 + 1) * 64],
                                      masked[:, g4 % 2, hh * 128:(hh + 1) * 128],
                                      v_tm[:, h * DH:(h + 1) * DH], start=True, stop=False)
                if KRET == "b":
                    def omm1b(eng, g4=g4, hh=hh, h=h):
                        return eng.matmul(pO[:, (h % 8) * 64:(h % 8 + 1) * 64],
                                          masked[:, g4 % 2, hh * 128:(hh + 1) * 128],
                                          v_tm[:, h * DH:(h + 1) * DH], start=True, stop=True)
                    sch.add("pe", omm1b, reads=("ropeA", "v_tm"), writes=("bank_pO",), tag=f"o{t}_oi{h}")
                else:
                    sch.add("pe", omm1, reads=("ropeA", "v_tm"), writes=("bank_pO",), tag=f"o{t}_oi{h}")

                    def omm2(eng, h=h):
                        return eng.matmul(pO[:, (h % 8) * 64:(h % 8 + 1) * 64],
                                          qs_bf[0:64, h, :], R_bf[0:64, h, :],
                                          start=False, stop=True)
                    sch.add("pe", omm2, reads=("qs_bf", "R_bf"), writes=("bank_pO",), tag=f"o{t}_ox{h}")

            if g4 % 2 == 1 and KRET not in ("a",):
                half = g4 // 2

                def mf(eng, half=half):
                    return eng.tensor_tensor(m_tm[:, half * 512:(half + 1) * 512], pO[:, :],
                                             g_tm[:, half * 512:(half + 1) * 512], ALU.mult)
                sch.add("dve", mf, reads=("bank_pO", "g_tm"), writes=("m_tm", "bank_pO"), tag=f"o{t}_m{half}")

        if KRET == "d":
            state_update(v_tm[:, :], "v_tm", k_tm[:, :], "k_tm", f"o{t}")
        if STAGE == "ret":
            if t == 0 and KRET == "dumpR":
                def rdump3(eng):
                    return eng.tensor_copy(m_tm[0:64, 0:1024], R_bf.ap().rearrange("p h d -> p (h d)"))
                sch.add("dve", rdump3, reads=("R_bf",), writes=("m_tm",), tag="rdump3")
            if KRET == "a":
                dma("pool", out_t.rearrange("(t p) d -> p t d", p=128)[:, t, :],
                    masked.ap().rearrange("p c n -> p (c n)"), ("ropeA",),
                    ("ropeA", "out_done"), f"dbg{t}")
            else:
                dma("pool", out_t.rearrange("(t p) d -> p t d", p=128)[:, t, :],
                    m_tm[:, :], ("m_tm",), ("m_tm", "out_done"), f"dbg{t}")
            return

        transpose8(m_tm[:, :], "m_tm", mfm[:, :], "q_tm", f"o{t}mT")
        proj("wo", mfm[:, :], "q_tm", f"o{t}wo")

        def aev(eng):
            eng.activation(ln_scr[:, 0:512], pA[:, :], AF.Identity)
            return eng.activation(ln_scr[:, 512:1024], pB[:, :], AF.Identity)
        sch.add("act", aev, reads=("bank_pA", "bank_pB"),
                writes=("ropeB", "bank_pA", "bank_pB"), tag=f"o{t}_aev")
        rstd2, nmr2, rres2 = ln_stats(ln_scr[:, :], "ropeB", 2 + buf, f"o{t}l2")

        def f_ap2(eng, rstd2=rstd2, nmr2=nmr2):
            return eng.activation(ropeA[:, :], ln_scr[:, :], AF.Identity, bias=nmr2, scale=rstd2)
        sch.add("act", f_ap2, reads=("ropeB", rres2), writes=("ropeA",), tag=f"o{t}_ap2")

        def f_add2(eng, xa=xa):
            return eng.tensor_add(xa, xa, ropeA[:, :])
        sch.add("dve", f_add2, reads=("ropeA", xr), writes=(xr,), tag=f"o{t}_res")

        rstd3, nmr3, rres3 = ln_stats(xa, xr, 4 + buf, f"o{t}l3")
        ln_apply(ks_bf[:, :], "ks_bf", xa, xr, rstd3, nmr3, rres3, f"o{t}l3")
        transpose8(ks_bf[:, :], "ks_bf",
                   Xln2_fm[:, :, t * 128:(t + 1) * 128],
                   f"Xln2_{t}", f"o{t}x2T")
        dma("sp", xscr.rearrange("(t p) d -> p t d", p=128)[:, t, :], xa,
            (xr,), (xr, f"xscr_{t}"), f"spill{t}")


    if STAGE not in ("pre", "pre2"):
        nt_run = min(NT, KOWNT)
        if nt_run:
            st = own_A(0)
            for t in range(nt_run):
                stn = own_A(t + 1) if t + 1 < nt_run else None
                own_B(t, st)
                st = stn
    sch.add("dve", lambda eng: eng.engine_nop(),
            reads=tuple(f"Xln2_{t}" for t in range(NT)), writes=("Xln2_all",), tag="x2all")

    if STAGE == "attn":
        for t in range(NT):
            dma("sp", xtile[:, 0, :], xscr.rearrange("(t p) d -> p t d", p=128)[:, t, :],
                ("xtile0", f"xscr_{t}"), ("xtile0",), f"dbgld{t}")
            dma("sp", out_t.rearrange("(t p) d -> p t d", p=128)[:, t, :],
                xtile[:, 0, :], ("xtile0",), ("xtile0", "out_done"), f"dbgst{t}")
    for t in (range(NT) if STAGE == "full" else ()):
        for ch in range(8):
            def mm(eng, ch=ch, t=t):
                return eng.matmul(pA[:, 0:E], Xln2_fm[:, ch, t * 128:(t + 1) * 128],
                                  gateT[:, ch, :], start=(ch == 0), stop=(ch == 7))
            sch.add("pe", mm, reads=(f"Xln2_{t}", "gateT"), writes=("bank_pA",), tag=f"g{t}_{ch}")

        nmx, sm, rs = gsc[:, 40:41], gsc[:, 41:42], gsc[:, 42:43]

        def f_mx(eng):
            return eng.reduce_max(nmx, pA[:, 0:E], axis=AX.X, negate=True)
        sch.add("dve", f_mx, reads=("bank_pA",), writes=("gmx",), tag=f"g{t}_mx", slow=True)

        def f_exp(eng):
            return eng.activation(s2t[:, :], pA[:, 0:E], AF.Exp, bias=nmx, accum_out=sm)
        sch.add("act", f_exp, reads=("bank_pA", "gmx"), writes=("gexp", "bank_pA"), tag=f"g{t}_exp")

        def f_rp(eng):
            return eng.reciprocal(rs, sm)
        sch.add("dve", f_rp, reads=("gexp",), writes=("grs",), tag=f"g{t}_rp", slow=True)

        def f_p(eng, t=t):
            return eng.tensor_scalar(p_all[:, t, :], s2t[:, :], rs, None, ALU.mult)
        sch.add("dve", f_p, reads=("gexp", "grs"), writes=("p_all", "gexp"), tag=f"g{t}_p")

        def f_r0(eng, t=t):
            return eng.tensor_add(s2t[:, :], p_all[:, t, :], gb_bcast[:, :])
        sch.add("dve", f_r0, reads=("p_all", "gbb_r"), writes=("grt0",), tag=f"g{t}_r0")

        def f_r1(eng):
            s2v = s2t.ap().rearrange("p (g e) -> p g e", g=4)
            m1v = gsc[:, 0:4].rearrange("p (g o) -> p g o", o=1)
            return eng.tensor_reduce(m1v, s2v, AX.X, ALU.max)
        sch.add("dve", f_r1, reads=("grt0",), writes=("grt1",), tag=f"g{t}_r1")

        def f_r2a(eng):
            s2v = s2t.ap().rearrange("p (g e) -> p g e", g=4)
            m1v = gsc[:, 0:4].rearrange("p (g o) -> p g o", o=1)
            eqt = gsc[:, 16:32].rearrange("p (g e) -> p g e", g=4)
            return eng.tensor_tensor(eqt, s2v, m1v.broadcast_to([128, 4, 4]), ALU.is_ge)
        sch.add("dve", f_r2a, reads=("grt1", "grt0"), writes=("grt2a",), tag=f"g{t}_r2a")

        def f_r2b(eng):
            s2v = s2t.ap().rearrange("p (g e) -> p g e", g=4)
            eqt = gsc[:, 16:32].rearrange("p (g e) -> p g e", g=4)
            return eng.scalar_tensor_tensor(eqt, eqt, -1e30, s2v, ALU.mult, ALU.add)
        sch.add("dve", f_r2b, reads=("grt2a", "grt0"), writes=("grt2b",), tag=f"g{t}_r2b")

        def f_r2c(eng):
            eqt = gsc[:, 16:32].rearrange("p (g e) -> p g e", g=4)
            m2v = gsc[:, 4:8].rearrange("p (g o) -> p g o", o=1)
            return eng.tensor_reduce(m2v, eqt, AX.X, ALU.max)
        sch.add("dve", f_r2c, reads=("grt2b",), writes=("grt2",), tag=f"g{t}_r2c")

        def f_r3a(eng):
            return eng.tensor_add(gsc[:, 8:12], gsc[:, 0:4], gsc[:, 4:8])
        sch.add("dve", f_r3a, reads=("grt2", "grt1"), writes=("grt3a",), tag=f"g{t}_r3a")

        def f_r3b(eng):
            return eng.tensor_reduce(gsc[:, 12:13], gsc[:, 8:12], AX.X, ALU.max)
        sch.add("dve", f_r3b, reads=("grt3a",), writes=("grt3",), tag=f"g{t}_r3b")

        def f_r4a(eng):
            return eng.tensor_scalar(gsc[:, 40:44], gsc[:, 8:12], gsc[:, 12:13], None, ALU.is_ge)
        sch.add("dve", f_r4a, reads=("grt3", "grt3a"), writes=("grt4a",), tag=f"g{t}_r4a")

        def f_r4b(eng):
            return eng.scalar_tensor_tensor(gsc[:, 40:44], gsc[:, 40:44], -1e30, gsc[:, 8:12], ALU.mult, ALU.add)
        sch.add("dve", f_r4b, reads=("grt4a", "grt3a"), writes=("grt4b",), tag=f"g{t}_r4b")

        def f_r4c(eng):
            return eng.tensor_reduce(gsc[:, 13:14], gsc[:, 40:44], AX.X, ALU.max)
        sch.add("dve", f_r4c, reads=("grt4b",), writes=("grt4",), tag=f"g{t}_r4c")

        def f_r5a(eng):
            return eng.tensor_scalar(gsc[:, 0:4], gsc[:, 8:12], gsc[:, 13:14], None, ALU.is_ge)
        sch.add("dve", f_r5a, reads=("grt4", "grt3a", "grt1"), writes=("grt5a",), tag=f"g{t}_r5a")

        def f_r5b(eng):
            return eng.tensor_scalar(gsc[:, 0:4], gsc[:, 0:4], 1.0, 1e30, ALU.subtract, ALU.mult)
        sch.add("dve", f_r5b, reads=("grt5a",), writes=("grt5b",), tag=f"g{t}_r5b")

        def f_r5c(eng):
            s2v = s2t.ap().rearrange("p (g e) -> p g e", g=4)
            keepv = gsc[:, 0:4].rearrange("p (g o) -> p g o", o=1)
            return eng.tensor_tensor(s2v, s2v, keepv.broadcast_to([128, 4, 4]), ALU.add)
        sch.add("dve", f_r5c, reads=("grt5b", "grt2b"), writes=("grt5c",), tag=f"g{t}_r5c")

        def f_r5d(eng):
            return eng.max(gsc[:, 16:24], s2t[:, :])
        sch.add("dve", f_r5d, reads=("grt5c", "grt2a"), writes=("grt5",), tag=f"g{t}_r5d")

        def f_r6(eng):
            return eng.tensor_scalar(gsc[:, 24:40], s2t[:, :], gsc[:, 17:18], None, ALU.is_ge)
        sch.add("dve", f_r6, reads=("grt5", "grt5c"), writes=("grt6",), tag=f"g{t}_r6")

        def f_route(eng, t=t):
            return eng.tensor_tensor(comb_t[:, :], gsc[:, 24:40], p_all[:, t, :], ALU.mult)
        sch.add("dve", f_route, reads=("grt6", "p_all"), writes=("comb",), tag=f"g{t}_rt")

        def f_ct(eng):
            return eng.transpose(pT[0:16, 0, :], comb_t[:, :], ident[:, :])
        sch.add("pe", f_ct, reads=("comb", "ident"), writes=("bank_pT",), tag=f"g{t}_ct")

        def f_cev(eng, t=t):
            return eng.tensor_copy(combine_fm[:, t * 128:(t + 1) * 128], pT[0:16, 0, :])
        sch.add("dve", f_cev, reads=("bank_pT",), writes=("combine_fm", "bank_pT"), tag=f"g{t}_cev")

    sch.barrier()
    rounds = ([[0, 1, 2, 3], [4, 5, 6, 7], [8, 9, 10, 11], [12, 13, 14, 15], [16, 17]]
              if STAGE == "full" else [])
    load_seq = 0

    def ffn_weight_load(e, tag):
        nonlocal load_seq
        s1 = load_seq % 2
        s2_ = load_seq % 4
        load_seq += 1
        if e < E:
            src1 = We1.rearrange("e (c p) i -> e p c i", p=128)[e, :, :, :]
            src3 = We3.rearrange("e (c p) i -> e p c i", p=128)[e, :, :, :]
            src2 = We2.rearrange("e (c p) d -> e p c d", p=128)[e, :, :, :]
        else:
            hf = e - E
            src1 = ws1.rearrange("(c p) j -> p c j", p=128)[:, :, hf * 512:(hf + 1) * 512]
            src3 = ws3.rearrange("(c p) j -> p c j", p=128)[:, :, hf * 512:(hf + 1) * 512]
            src2 = ws2.rearrange("(c p) d -> p c d", p=128)[:, 4 * hf:4 * (hf + 1), :]
        dma("pool", wr1[:, s1, :, :], src1, (f"wr1_{s1}",), (f"wr1_{s1}",), f"{tag}w1")
        dma("pool", wr3[:, s1, :, :], src3, (f"wr3_{s1}",), (f"wr3_{s1}",), f"{tag}w3")
        dma("pool", wr2[:, s2_, :, :], src2, (f"wr2_{s2_}",), (f"wr2_{s2_}",), f"{tag}w2")
        return s1, s2_

    pS0v = pS0
    pS1v = pS1
    for gi in (range(2) if STAGE == "full" else ()):
        tok0 = gi * 1024
        for ri, rnd in enumerate(rounds):
            w2slots = []
            for ei, e in enumerate(rnd):
                s1, s2_ = ffn_weight_load(e, f"f{gi}r{ri}e{e}")
                w2slots.append(s2_)
                is_sh = e >= E

                if not is_sh:
                    for half, pb, pres in ((0, pA, "bank_pA"), (1, pB, "bank_pB")):
                        def cbm(eng, e=e, half=half, pb=pb, tok0=tok0):
                            return eng.matmul(pb[:, :], onehot_sb[:, e * 128:(e + 1) * 128],
                                              combine_fm[:, tok0 + half * 512:tok0 + (half + 1) * 512],
                                              start=True, stop=True)
                        sch.add("pe", cbm, reads=("onehot", "combine_fm"), writes=(pres,), tag=f"f{gi}e{e}cb{half}")

                    def cbe(eng):
                        eng.tensor_copy(cb_bf[:, 0:512], pA[:, :])
                        return eng.tensor_copy(cb_bf[:, 512:1024], pB[:, :])
                    sch.add("dve", cbe, reads=("bank_pA", "bank_pB"),
                            writes=("cb_bf", "bank_pA", "bank_pB"), tag=f"f{gi}e{e}cbe")

                nich = 4
                for ich in range(nich):
                    HQ = [[(pA, "bank_pA"), (pB, "bank_pB"), (pS0v, "bank_pS0"), (pS1v, "bank_pS1")],
                          [(pO, "bank_pO"), (pT_f2, "bank_pT"), (pQT_f, "bank_pQT"), (pKT_f, "bank_pKT")]]
                    quad = HQ[ich % 2]
                    for tp in range(2):
                        wsrc = wr1 if tp == 0 else wr3
                        wres = f"wr1_{s1}" if tp == 0 else f"wr3_{s1}"
                        for ch in range(8):
                            for th in range(2):
                                pb, pres = quad[tp * 2 + th]
                                def hmm(eng, ich=ich, ch=ch, th=th, pb=pb, wsrc=wsrc, s1=s1, tok0=tok0):
                                    return eng.matmul(pb[:, :],
                                                      wsrc[:, s1, ch, ich * 128:(ich + 1) * 128],
                                                      Xln2_fm[:, ch, tok0 + th * 512:tok0 + (th + 1) * 512],
                                                      start=(ch == 0), stop=(ch == 7))
                                sch.add("pe", hmm, reads=(wres, "Xln2_all"), writes=(pres,),
                                        tag=f"f{gi}e{e}h{tp}_{ich}_{th}_{ch}")

                    if is_sh:
                        bcol1 = bs1T[:, (e - E) * 4 + ich:(e - E) * 4 + ich + 1]
                        bcol3 = bs3T[:, (e - E) * 4 + ich:(e - E) * 4 + ich + 1]
                        bres1, bres3 = "bs1T", "bs3T"
                    else:
                        bcol1 = biasT[:, ich, e:e + 1]
                        bcol3 = bias3T[:, ich, e:e + 1]
                        bres1, bres3 = "be1T", "be3T"

                    def sil(eng, bcol1=bcol1, quad=quad):
                        eng.activation(h1s[:, 0:512], quad[0][0][:, :], AF.Silu, bias=bcol1)
                        return eng.activation(h1s[:, 512:1024], quad[1][0][:, :], AF.Silu, bias=bcol1)
                    sch.add("act", sil, reads=(quad[0][1], quad[1][1], bres1),
                            writes=("h1s", quad[0][1], quad[1][1]), tag=f"f{gi}e{e}sil{ich}")

                    if not is_sh:
                        def hcf(eng):
                            return eng.tensor_tensor(hcm[:, :], h1s[:, :], cb_bf[:, :], ALU.mult)
                        sch.add("dve", hcf, reads=("h1s", "cb_bf"), writes=("hcm",), tag=f"f{gi}e{e}hc{ich}")
                        hsrc, hres = hcm, "hcm"
                    else:
                        hsrc, hres = h1s, "h1s"

                    def hfin(eng, ich=ich, ei=ei, bcol3=bcol3, hsrc=hsrc, quad=quad):
                        eng.scalar_tensor_tensor(h4[:, ei, ich, 0:512], quad[2][0][:, :],
                                                 bcol3, hsrc[:, 0:512], ALU.add, ALU.mult)
                        return eng.scalar_tensor_tensor(h4[:, ei, ich, 512:1024], quad[3][0][:, :],
                                                        bcol3, hsrc[:, 512:1024], ALU.add, ALU.mult)
                    sch.add("dve", hfin, reads=(quad[2][1], quad[3][1], hres, bres3),
                            writes=(f"h4_{ei}", quad[2][1], quad[3][1]), tag=f"f{gi}e{e}hf{ich}")

            for tl in range(8):
                par = tl % 2
                t_abs = gi * 8 + tl
                for ei, e in enumerate(rnd):
                    s2_ = w2slots[ei]
                    for ich in range(4):
                        for dhh in range(2):
                            pb, pres = YB[par][dhh]

                            def ymm(eng, ei=ei, tl=tl, ich=ich, dhh=dhh, s2_=s2_, pb=pb,
                                    start=(ei == 0 and ich == 0),
                                    stop=(ri < 4 and ei == len(rnd) - 1 and ich == 3)):
                                return eng.matmul(pb[:, :],
                                                  h4[:, ei, ich, tl * 128:(tl + 1) * 128],
                                                  wr2[:, s2_, ich, dhh * 512:(dhh + 1) * 512],
                                                  start=start, stop=stop)
                            sch.add("pe", ymm, reads=(f"h4_{ei}", f"wr2_{s2_}"), writes=(pres,),
                                    tag=f"f{gi}r{ri}yo{tl}_{ei}_{ich}_{dhh}")

                if ri == 4:
                    def bem(eng, tl=tl, t_abs=t_abs, par=par):
                        eng.matmul(YB[par][0][0][:, :],
                                   combine_fm[:, t_abs * 128:(t_abs + 1) * 128], be2_sb[:, 0:512],
                                   start=False, stop=False)
                        eng.matmul(YB[par][1][0][:, :], combine_fm[:, t_abs * 128:(t_abs + 1) * 128],
                                   be2_sb[:, 512:1024], start=False, stop=False)
                        eng.matmul(YB[par][0][0][:, :], ones_bf[0:1, :], bs2_sb[0:1, 0:512],
                                   start=False, stop=True)
                        return eng.matmul(YB[par][1][0][:, :], ones_bf[0:1, :], bs2_sb[0:1, 512:1024],
                                          start=False, stop=True)
                    sch.add("pe", bem, reads=("combine_fm", "be2", "bs2", "ones"),
                            writes=(YB[par][0][1], YB[par][1][1]), tag=f"f{gi}be2_{tl}")

                def yev(eng, tl=tl, par=par, ri=ri):
                    if ri == 0:
                        eng.tensor_copy(y_sb[:, tl, 0:512], YB[par][0][0][:, :])
                        return eng.tensor_copy(y_sb[:, tl, 512:1024], YB[par][1][0][:, :])
                    eng.tensor_tensor(y_sb[:, tl, 0:512], y_sb[:, tl, 0:512], YB[par][0][0][:, :], ALU.add)
                    return eng.tensor_tensor(y_sb[:, tl, 512:1024], y_sb[:, tl, 512:1024], YB[par][1][0][:, :], ALU.add)
                sch.add("dve", yev, reads=(YB[par][0][1], YB[par][1][1], f"ysb_{tl}"),
                        writes=(f"ysb_{tl}", YB[par][0][1], YB[par][1][1]), tag=f"f{gi}r{ri}yev{tl}")

                if ri == 4:
                    rstd4, nmr4, rres4 = ln_stats(y_sb[:, tl, :], f"ysb_{tl}", 6 + par, f"f{gi}t{tl}l4")

                    def f_ap4(eng, tl=tl, par=par, rstd4=rstd4, nmr4=nmr4):
                        eng.activation(YB[par][0][0][:, :], y_sb[:, tl, 0:512], AF.Identity,
                                       bias=nmr4, scale=rstd4)
                        return eng.activation(YB[par][1][0][:, :], y_sb[:, tl, 512:1024], AF.Identity,
                                              bias=nmr4, scale=rstd4)
                    sch.add("act", f_ap4, reads=(f"ysb_{tl}", rres4, YB[par][0][1], YB[par][1][1]),
                            writes=("lnf", YB[par][0][1], YB[par][1][1]), tag=f"f{gi}t{tl}ap4")

                    dma("sp", xres[:, 0, :], xscr.rearrange("(t p) d -> p t d", p=128)[:, t_abs, :],
                        ("xres0", f"xscr_{t_abs}"), ("xres0",), f"f{gi}xld{tl}")

                    def f_add(eng, par=par):
                        eng.tensor_tensor(xres[:, 0, 0:512], xres[:, 0, 0:512], YB[par][0][0][:, :], ALU.add)
                        return eng.tensor_tensor(xres[:, 0, 512:1024], xres[:, 0, 512:1024], YB[par][1][0][:, :], ALU.add)
                    sch.add("dve", f_add, reads=("lnf", "xres0", YB[par][0][1], YB[par][1][1]),
                            writes=("xres0", YB[par][0][1], YB[par][1][1]), tag=f"f{gi}out{tl}")

                    dma("sp", out_t.rearrange("(t p) d -> p t d", p=128)[:, t_abs, :],
                        xres[:, 0, :], ("xres0",), ("xres0", "out_done"), f"f{gi}st{tl}")

    sch.add("dve", lambda eng: eng.engine_nop(), reads=("out_done",), tag="final")

    globals()["DEBUG_TENSORS"] = dict(
        R_all=R_all, R_bf=R_bf, xtile=xtile)
    sch.finalize()
    sems = {}
    with ExitStack() as ctx:
        Ectx = ctx.enter_context
        for e in Sched.ENG:
            sems[e] = Ectx(nc.semaphore(f"sem_{e}"))
        for e in ("sp", "pool"):
            for q in range(Sched.NLANES):
                sems[f"{e}.q{q}"] = Ectx(nc.semaphore(f"sem_{e}_q{q}"))
        block = Ectx(nc.Block())

        @block.tensor
        def _(eng):
            sch.run_engine("pe", eng, sems)

        @block.vector
        def _(eng):
            sch.run_engine("dve", eng, sems)

        @block.scalar
        def _(eng):
            sch.run_engine("act", eng, sems)

        @block.gpsimd
        def _(eng):
            sch.run_engine("pool", eng, sems)

        @block.sync
        def _(eng):
            sch.run_engine("sp", eng, sems)
            for e in Sched.ENG:
                if e != "sp" and sch.final_counts.get(e):
                    eng.wait_ge(sems[e], sch.final_counts[e])
            for qe in ("sp", "pool"):
                nq = sch.lane_use[qe]
                for q in range(Sched.NLANES):
                    uses = nq // Sched.NLANES + (1 if q < nq % Sched.NLANES else 0)
                    if uses:
                        eng.wait_ge(sems[f"{qe}.q{q}"], 16 * uses)

    return nc


def _tables(core):
    pos_chunk = core % 4
    base = pos_chunk * TCORE
    inv = 1.0 / ROPE_BASE ** (np.arange(0, DH, 2, dtype=np.float64) / DH)
    gam = 1.0 - np.exp2(-5.0 - np.arange(H, dtype=np.float64))

    def rope_tabs(positions):
        th = positions[:, :, None] * inv[None, None, :]
        cos = np.cos(th)
        cos = np.concatenate([cos, cos], axis=2)
        sin = np.sin(th)
        sin_signed = np.concatenate([-sin, sin], axis=2)
        return cos.astype(np.float32), sin_signed.astype(np.float32)

    p = np.arange(128)
    own_pos = base + np.arange(NT)[None, :] * 128 + p[:, None]
    cos_o, sin_o = rope_tabs(own_pos)
    pre_pos = np.arange(NPT)[None, :] * 128 + p[:, None] - (PRE - base)
    pre_pos = np.maximum(pre_pos, 0)
    cos_p, sin_p = rope_tabs(pre_pos)
    cos_p = cos_p.astype(ml_dtypes.bfloat16)
    sin_p = sin_p.astype(ml_dtypes.bfloat16)

    i = np.arange(128)
    j = np.arange(128)
    diff = i[None, :] - j[:, None]
    dec = np.where(diff >= 0, gam[:, None, None] ** np.maximum(diff, 0)[None], 0.0) / 8.0
    decayT = np.transpose(dec, (1, 0, 2)).astype(ml_dtypes.bfloat16)

    gpow = np.zeros((128, D), dtype=np.float64)
    for h in range(H):
        gpow[:, h * DH:(h + 1) * DH] = (gam[h] ** (127 - i))[:, None]
    gpow = gpow.astype(ml_dtypes.bfloat16)

    g128 = np.zeros((64, H * DH), dtype=np.float64)
    for h in range(H):
        g128[:, h * DH:(h + 1) * DH] = gam[h] ** 128
    g128 = g128.astype(np.float32)

    gb = np.zeros((64, H, 128), dtype=np.float64)
    for h in range(H):
        gb[:, h, :] = gam[h] ** (i[None, :] + 1) / 8.0
    gbcast = gb.astype(ml_dtypes.bfloat16)

    onehot = np.zeros((16, 16 * 128), dtype=ml_dtypes.bfloat16)
    for e in range(16):
        onehot[e, e * 128:(e + 1) * 128] = 1.0

    return dict(cos_own=cos_o, sin_own=sin_o, cos_pre=cos_p, sin_pre=sin_p,
                decayT=decayT, gpow_k=gpow, g128=g128, gbcast=gbcast, onehot=onehot,
                ident_in=np.eye(128, dtype=ml_dtypes.bfloat16))


_NC_CACHE = {}


def kernel(L0_attn_x, L0_attn_wq, L0_attn_wk, L0_attn_wv, L0_attn_wg, L0_attn_wo,
           L0_ffn_gate_w, L0_ffn_gate_b, L0_ffn_We1, L0_ffn_be1, L0_ffn_We2,
           L0_ffn_be2, L0_ffn_We3, L0_ffn_be3, L0_ffn_ws1, L0_ffn_bs1,
           L0_ffn_ws2, L0_ffn_bs2, L0_ffn_ws3, L0_ffn_bs3):
    X = np.ascontiguousarray(np.asarray(L0_attn_x, dtype=np.float32))
    shared = dict(
        wq=np.ascontiguousarray(L0_attn_wq), wk=np.ascontiguousarray(L0_attn_wk),
        wv=np.ascontiguousarray(L0_attn_wv), wg=np.ascontiguousarray(L0_attn_wg),
        wo=np.ascontiguousarray(L0_attn_wo),
        gate_w=np.ascontiguousarray(L0_ffn_gate_w),
        gate_b=np.ascontiguousarray(L0_ffn_gate_b).reshape(1, E),
        We1=np.ascontiguousarray(L0_ffn_We1), be1=np.ascontiguousarray(L0_ffn_be1),
        We2=np.ascontiguousarray(L0_ffn_We2), be2=np.ascontiguousarray(L0_ffn_be2),
        We3=np.ascontiguousarray(L0_ffn_We3), be3=np.ascontiguousarray(L0_ffn_be3),
        ws1=np.ascontiguousarray(L0_ffn_ws1), bs1=np.ascontiguousarray(L0_ffn_bs1).reshape(1, D),
        ws2=np.ascontiguousarray(L0_ffn_ws2), bs2=np.ascontiguousarray(L0_ffn_bs2).reshape(1, D),
        ws3=np.ascontiguousarray(L0_ffn_ws3), bs3=np.ascontiguousarray(L0_ffn_bs3).reshape(1, D),
    )
    in_maps = []
    for c in range(NCORE):
        b, pos = c // 4, c % 4
        base = pos * TCORE
        xp = np.zeros((PRE, D), dtype=np.float32)
        if base > 0:
            xp[PRE - base:] = X[b, 0:base]
        m = dict(shared)
        m["x_own"] = np.ascontiguousarray(X[b, base:base + TCORE])
        m["x_pre"] = xp
        m.update(_tables(c))
        in_maps.append(m)

    if "nc" not in _NC_CACHE:
        _NC_CACHE["nc"] = build_nc()
    nc = _NC_CACHE["nc"]
    res = run_bass_kernel_spmd(nc, in_maps, core_ids=list(range(NCORE)))
    out = np.empty((B, S, D), dtype=np.float32)
    for c in range(NCORE):
        b, pos = c // 4, c % 4
        out[b, pos * TCORE:(pos + 1) * TCORE] = res.results[c]["out"]
    return out
